# version 29
# baseline (speedup 1.0000x reference)
import numpy as np
import ml_dtypes
import concourse.bass as bass
import concourse.mybir as mybir
from concourse.bass_utils import run_bass_kernel_spmd

F32 = mybir.dt.float32
BF16 = mybir.dt.bfloat16
AF = mybir.ActivationFunctionType
ALU = mybir.AluOpType

D = 2048
NH = 16
HD = 128
T = 4096
TQ = 2048
INTER = 5632
DC = D // 128
ICN = INTER // 128
G_P1 = 4
TG1 = 1024
NQT = 4
NTT = 4
NFG = 4
EXP_SCALE = 1.0 / float(np.sqrt(128.0))
EXP_BIAS = -12.0
EPS = 1e-5
INV_D = 1.0 / 2048.0

A_COS = 0
A_SIN = 4096
A_K = 8192
A_V = 12288
A_Q = 16384
A_XN = 49152
A_H1N = 0
A_ACT = 16384
WD_SLOTS = [49152, 54784, 38912]
A_WRING = 65536
A_EXPP = 73728
A_H1BF = 75776
A_H1SQ = 76800
A_AVSB = 77824
A_NEGC = 75776
ARENA_N = 78848

CONSTS = [0.0, 1.0, INV_D, -0.5, -1.0, EPS, EXP_BIAS, EXP_SCALE, -800.0, 0.5]

LAST_PLAN = None
ENGS = ("tensor", "vector", "scalar", "sync", "gpsimd")
ESEM = {"tensor": "pe", "vector": "dve", "scalar": "act", "sync": "sy",
        "gpsimd": "ld"}
COMPUTE_SEMS = ("pe", "dve", "act")
ALL_SEMS = ("pe", "dve", "act", "sy", "ld", "st")


class Plan:
    def __init__(self):
        self.ops = {e: [] for e in ENGS}
        self.counts = {s: 0 for s in ("pe", "dve", "act")}
        self.events = {}
        self.missing = []

    def op(self, eng, fn, deps=(), sig=None, dma=None):
        inc = None
        if sig is not None or dma is not None:
            if dma is not None:
                sname, amt = dma, 16
            else:
                sname, amt = ESEM[eng], 1
            self.counts[sname] = self.counts.get(sname, 0) + amt
            inc = (sname, amt)
            if sig is not None:
                labels = sig if isinstance(sig, (list, tuple)) else [sig]
                for lb in labels:
                    assert lb not in self.events, f"duplicate label {lb}"
                    self.events[lb] = (sname, self.counts[sname])
        self.ops[eng].append(([d for d in deps if d is not None], fn, inc))

    def snapshot_deps(self):
        out = []
        for s in list(self.counts):
            lbl = f"_bar_{s}_{len(self.ops['sync'])}_{len(self.ops['tensor'])}_{self.counts[s]}"
            self.events[lbl] = (s, self.counts[s])
            out.append(lbl)
        return out

    def phase_barrier(self):
        deps = self.snapshot_deps()
        for e in ("tensor", "vector", "scalar"):
            self.op(e, lambda eng: eng.nop(), deps=deps)

    def finalize(self):
        out = {}
        for eng in ENGS:
            last = {}
            lst = []
            for deps, fn, inc in self.ops[eng]:
                waits = []
                for d in deps:
                    if d not in self.events:
                        self.missing.append((eng, d))
                        continue
                    s, v = self.events[d]
                    if s in COMPUTE_SEMS and s == ESEM[eng]:
                        continue
                    if last.get(s, -1) < v:
                        waits.append((s, v))
                        last[s] = v
                lst.append((waits, fn, inc))
            out[eng] = lst
        return out


def build_program(debug=False, rc_detect=True):
    nc = bass.Bass("TRN2", target_bir_lowering=False,
                   detect_race_conditions=rc_detect)

    xt = nc.dram_tensor("xt", [D, T], F32, kind="ExternalInput").ap()
    cosb = nc.dram_tensor("cosb", [HD, T], BF16, kind="ExternalInput").ap()
    sinb = nc.dram_tensor("sinb", [HD, T], BF16, kind="ExternalInput").ap()
    wq_p = nc.dram_tensor("wq_p", [18, 128, D], F32, kind="ExternalInput").ap()
    wo_p = nc.dram_tensor("wo_p", [16, 128, D], F32, kind="ExternalInput").ap()
    wg_p = nc.dram_tensor("wg_p", [ICN, 128, D], F32, kind="ExternalInput").ap()
    wu_p = nc.dram_tensor("wu_p", [ICN, 128, D], F32, kind="ExternalInput").ap()
    wd_p = nc.dram_tensor("wd_p", [16, 128, INTER], F32, kind="ExternalInput").ap()
    gs_d = nc.dram_tensor("gs_d", [128, ICN], F32, kind="ExternalInput").ap()
    us_d = nc.dram_tensor("us_d", [128, ICN], F32, kind="ExternalInput").ap()
    ds_d = nc.dram_tensor("ds_d", [128, 16], F32, kind="ExternalInput").ap()
    cvals = nc.dram_tensor("cvals", [128, len(CONSTS)], F32, kind="ExternalInput").ap()
    onec_d = nc.dram_tensor("onec_d", [128, 1], BF16, kind="ExternalInput").ap()
    ones128_d = nc.dram_tensor("ones128_d", [128, 128], BF16, kind="ExternalInput").ap()
    ident_d = nc.dram_tensor("ident_d", [128, 128], BF16, kind="ExternalInput").ap()
    oner_d = nc.dram_tensor("oner_d", [1, 128], BF16, kind="ExternalInput").ap()

    yt = nc.dram_tensor("yt", [D, TQ], F32, kind="ExternalOutput").ap()
    cdram = nc.dram_tensor("cdram", [64, 512], BF16, kind="Internal").ap()
    wso = nc.dram_tensor("wso", [16, 128, D], BF16, kind="Internal").ap()
    wsg = nc.dram_tensor("wsg", [ICN, 128, D], BF16, kind="Internal").ap()
    wsu = nc.dram_tensor("wsu", [ICN, 128, D], BF16, kind="Internal").ap()
    wsd = nc.dram_tensor("wsd", [16, 128, INTER], BF16, kind="Internal").ap()
    h1T = nc.dram_tensor("h1T", [D, TQ], F32,
                         kind="ExternalOutput" if debug else "Internal").ap()
    if debug:
        d_qrd = nc.dram_tensor("d_qrd", [64 * 128, 512], BF16, kind="ExternalOutput").ap()
        d_rcp = nc.dram_tensor("d_rcp", [64, 512], F32, kind="ExternalOutput").ap()
        d_av = nc.dram_tensor("d_av", [64 * 128, 512], BF16, kind="ExternalOutput").ap()
        d_k = nc.dram_tensor("d_k", [128, T], BF16, kind="ExternalOutput").ap()
        d_v = nc.dram_tensor("d_v", [128, T], BF16, kind="ExternalOutput").ap()
        d_q = nc.dram_tensor("d_q", [128, 32768], BF16, kind="ExternalOutput").ap()
        d_xn = nc.dram_tensor("d_xn", [128, 16384], BF16, kind="ExternalOutput").ap()
        d_att = nc.dram_tensor("d_att", [128, 32768], BF16, kind="ExternalOutput").ap()

    from contextlib import ExitStack
    with ExitStack() as stk:
        E = stk.enter_context
        cpool = E(nc.sbuf_tensor("cpool", [128, len(CONSTS)], F32))
        for j, v in enumerate(CONSTS):
            nc.const_aps.register(F32, v, cpool[:, j:j + 1])
        arena = E(nc.sbuf_tensor("arena", [128, ARENA_N], BF16))
        gs = E(nc.sbuf_tensor("gs", [128, ICN], F32))
        us = E(nc.sbuf_tensor("us", [128, ICN], F32))
        ds = E(nc.sbuf_tensor("ds", [128, 16], F32))
        onec = E(nc.sbuf_tensor("onec", [128, 1], BF16))
        ones128 = E(nc.sbuf_tensor("ones128", [128, 128], BF16))
        ident = E(nc.sbuf_tensor("ident", [128, 128], BF16))
        oner = E(nc.sbuf_tensor("oner", [1, 128], BF16))
        xres = E(nc.sbuf_tensor("xres", [128, 1024], F32))
        h1sb = E(nc.sbuf_tensor("h1sb", [128, 1536], F32))
        h1r = E(nc.sbuf_tensor("h1r", [128, 3072], F32))
        ysb = E(nc.sbuf_tensor("ysb", [128, 1024], F32))
        ropeA = E(nc.sbuf_tensor("ropeA", [128, 512], F32))
        ropeB = E(nc.sbuf_tensor("ropeB", [128, 512], F32))
        rows = E(nc.sbuf_tensor("rows", [1, 2048], F32))
        rows_bf = E(nc.sbuf_tensor("rows_bf", [1, 1024], BF16))
        ln2bf = E(nc.sbuf_tensor("ln2bf", [1, 4096], BF16))
        rcpbf = E(nc.sbuf_tensor("rcpbf", [1, 1024], BF16))
        negcbf = E(nc.sbuf_tensor("negcbf", [1, 1024], BF16))
        mkrow = E(nc.sbuf_tensor("mkrow", [1, 16], F32))

        scA = E(nc.psum_tensor("scA", [128, 1024], F32))
        scB = E(nc.psum_tensor("scB", [128, 1024], F32))
        av0 = E(nc.psum_tensor("av0", [128, 512], F32))
        av1 = E(nc.psum_tensor("av1", [128, 512], F32))
        smb = E(nc.psum_tensor("smb", [128, 512], F32))
        rcb = E(nc.psum_tensor("rcb", [128, 512], F32))

        P = Plan()

        def asl(off, n):
            return arena[:, off:off + n]

        inits = [
            (asl(A_COS, T), cosb), (asl(A_SIN, T), sinb),
            (gs[:], gs_d), (us[:], us_d), (ds[:], ds_d),
            (cpool[:], cvals), (onec[:], onec_d), (oner[:], oner_d),
            (ones128[:], ones128_d), (ident[:], ident_d),
        ]
        for i, (o, ii) in enumerate(inits):
            P.op("sync", lambda e, o=o, ii=ii: e.dma_start(out=o, in_=ii),
                 dma="init", sig=(f"init_{i}" if i == len(inits) - 1 else None))
        INIT = [f"init_{len(inits) - 1}"]
        for e in ("tensor", "vector", "scalar"):
            P.op(e, lambda eng: eng.nop(), deps=INIT)

        wring_free = [None, None, None, None]
        wring_seq = [0]

        def wload(src_ap, sig, extra=()):
            slot = wring_seq[0] % 4
            wring_seq[0] += 1
            fr = wring_free[slot]
            fr = fr if isinstance(fr, list) else [fr]
            P.op("gpsimd",
                 lambda e, s=slot, a=src_ap: e.dma_start(
                     out=asl(A_WRING + s * 2048, 2048), in_=a),
                 deps=fr + list(extra), dma=f"wr{slot}", sig=sig)
            return slot

        def wtile(slot, dc):
            return asl(A_WRING + slot * 2048 + dc * 128, 128)

        qkv_banks = [av0, av1]
        qkv_bank_free = [None, None]
        xsq_slots = [asl(A_H1BF + i * 512, 512) for i in range(4)]

        def xg_t(dc, th, w=512):
            return asl(A_XN + dc * TG1 + th * 512, w)

        bank_i = [0]

        def next_bank():
            bank_i[0] ^= 1
            return qkv_banks[bank_i[0]], bank_i[0]

        for g in range(G_P1):
            gc = g * TG1
            own = g < 2
            for th2 in range(2):
                xg_out = bass.AP(arena, A_XN + th2 * 512,
                                 [[ARENA_N, 128], [TG1, DC], [1, 512]])
                xg_in = xt.rearrange("(c p) t -> p c t", p=128)[
                    :, :, gc + th2 * 512:gc + th2 * 512 + 512]
                P.op("gpsimd",
                     lambda e, o=xg_out, ii=xg_in: e.dma_start(out=o, in_=ii),
                     deps=[f"vps_{g - 1}_{th2}_3" if th2 == 0
                           else f"qkvlast_{g - 1}"], dma=f"xg{th2}",
                     sig=f"xg_{g}_{th2}")

            def sq_lbl(n):
                if n < 0:
                    return None
                gg, rr = divmod(n, 2 * DC)
                tt_, dd = divmod(rr, DC)
                return f"p1sq_{gg}_{tt_}_{dd}"

            def p1_sq(th, dcs):
                nsq = (g * 2 + th) * DC
                for dc in dcs:
                    P.op("scalar",
                         lambda e, dc=dc, th=th: e.activation(
                             xsq_slots[dc % 4], xg_t(dc, th), AF.Square),
                         deps=[f"xg_{g}_{th}", sq_lbl(nsq + dc - 4)],
                         sig=f"p1aq_{g}_{th}_{dc}")

            def p1_stats(th):
                for dc in range(DC):
                    P.op("tensor",
                         lambda e, dc=dc, th=th: e.matmul(
                             scA[:, 0:512], ones128[:], xg_t(dc, th),
                             start=(dc == 0), stop=(dc == DC - 1)),
                         deps=([f"xg_{g}_{th}",
                                f"rowm_{g}_{th - 1}" if th else f"rowm_{g - 1}_1"]
                               if dc == 0 else []),
                         sig=(f"stx_{g}_{th}" if dc == DC - 1 else None))
                for dc in range(DC):
                    P.op("tensor",
                         lambda e, dc=dc, th=th: e.matmul(
                             scA[:, 512:1024], ones128[:], xsq_slots[dc % 4],
                             start=(dc == 0), stop=(dc == DC - 1)),
                         deps=[f"p1aq_{g}_{th}_{dc}"] +
                              ([f"rowv_{g}_{th - 1}" if th else f"rowv_{g - 1}_1"]
                               if dc == 0 else []),
                         sig=f"p1sq_{g}_{th}_{dc}")

            def p1_chain(th):
                P.op("scalar", lambda e: e.activation(
                        rows[:, 0:512], scA[0:1, 0:512], AF.Copy, scale=INV_D),
                     deps=[f"stx_{g}_{th}"], sig=f"rowm_{g}_{th}")
                P.op("scalar", lambda e: e.activation(
                        rows[:, 512:1024], rows[:, 0:512], AF.Square), deps=[],
                     sig=f"msq_{g}_{th}")
                P.op("vector", lambda e: e.scalar_tensor_tensor(
                        rows[:, 1024:1536], scA[0:1, 512:1024], INV_D,
                        rows[:, 512:1024], ALU.mult, ALU.subtract),
                     deps=[f"p1sq_{g}_{th}_{DC - 1}", f"msq_{g}_{th}"],
                     sig=f"rowv_{g}_{th}")
                P.op("scalar", lambda e: e.activation(
                        rows[:, 1536:2048], rows[:, 1024:1536], AF.Ln, bias=EPS),
                     deps=[f"rowv_{g}_{th}"])
                P.op("scalar", lambda e: e.activation(
                        rows[:, 512:1024], rows[:, 1536:2048], AF.Exp, scale=-0.5),
                     deps=[])
                P.op("scalar", lambda e: e.activation(
                        rows_bf[:, 0:512], rows[:, 0:512], AF.Copy),
                     deps=[f"bcr_{g}_{th - 1}" if th else f"bcr_{g - 1}_1"])
                P.op("scalar", lambda e: e.activation(
                        rows_bf[:, 512:1024], rows[:, 512:1024], AF.Copy),
                     deps=[], sig=f"rowbf_{g}_{th}")

            def p1_rest(th):
                P.op("tensor", lambda e: e.matmul(
                        scB[:, 0:512], oner[:], rows_bf[:, 0:512],
                        start=True, stop=True),
                     deps=[f"rowbf_{g}_{th}",
                           f"apl_{g}_{th - 1}" if th else f"apl_{g - 1}_1"],
                     sig=f"bcm_{g}_{th}")
                P.op("tensor", lambda e: e.matmul(
                        scB[:, 512:1024], oner[:], rows_bf[:, 512:1024],
                        start=True, stop=True),
                     deps=[], sig=f"bcr_{g}_{th}")
                for dc in range(DC):
                    P.op("vector", lambda e, dc=dc, th=th: e.tensor_sub(
                            xg_t(dc, th), xg_t(dc, th), scB[:, 0:512]),
                         deps=[f"bcm_{g}_{th}", f"p1sq_{g}_{th}_{DC - 1}",
                               f"stx_{g}_{th}"])
                    P.op("vector", lambda e, dc=dc, th=th: e.tensor_mul(
                            xg_t(dc, th), xg_t(dc, th), scB[:, 512:1024]),
                         deps=[f"bcr_{g}_{th}"],
                         sig=([f"apl_{g}_{th}_{dc}", f"apl_{g}_{th}"]
                              if dc == DC - 1 else f"apl_{g}_{th}_{dc}"))
                ocs_ = list(range(18)) if own else [16, 17]
                for oc in ocs_:
                    slot = wload(wq_p[oc], f"wq_{g}_{th}_{oc}")
                    if True:
                        if oc == 17:
                            for j in range(4):
                                bank, bi = next_bank()
                                is_last_grp = (th == 1 and j == 3)
                                for dc in range(DC):
                                    P.op("tensor",
                                         lambda e, dc=dc, th=th, j=j, b=bank, s=slot:
                                         e.matmul(
                                             b[:, 0:128],
                                             asl(A_XN + dc * TG1 + th * 512 + j * 128, 128),
                                             wtile(s, dc),
                                             start=(dc == 0), stop=(dc == DC - 1)),
                                         deps=([f"apl_{g}_{th}", f"wq_{g}_{th}_{oc}",
                                                qkv_bank_free[bi]] if dc == 0 else []),
                                         sig=((["qkvlast_%d" % g, f"vps_{g}_{th}_{j}"]
                                               if is_last_grp else f"vps_{g}_{th}_{j}")
                                              if dc == DC - 1 else None))
                                ev = f"vev_{g}_{th}_{j}"
                                P.op("vector",
                                     lambda e, th=th, j=j, b=bank, gc=gc: e.tensor_copy(
                                         asl(A_V + gc + th * 512 + j * 128, 128),
                                         b[:, 0:128]),
                                     deps=[f"vps_{g}_{th}_{j}"], sig=ev)
                                qkv_bank_free[bi] = ev
                        else:
                            bank, bi = next_bank()
                            first_oc = (oc == ocs_[0])
                            for dc in range(DC):
                                P.op("tensor",
                                     lambda e, dc=dc, th=th, b=bank, s=slot: e.matmul(
                                         b[:], wtile(s, dc), xg_t(dc, th),
                                         start=(dc == 0), stop=(dc == DC - 1)),
                                     deps=([f"apl_{g}_{th}_{dc}"] if first_oc else []) +
                                          (([f"apl_{g}_{th}", f"wq_{g}_{th}_{oc}",
                                             qkv_bank_free[bi]] if not first_oc else
                                            [f"wq_{g}_{th}_{oc}", qkv_bank_free[bi]])
                                           if dc == 0 else []),
                                     sig=(f"qps_{g}_{oc}_{th}" if dc == DC - 1 else None))
                            dst = (asl(A_Q + oc * TQ + gc + th * 512, 512) if oc < 16
                                   else asl(A_K + gc + th * 512, 512))
                            csl = asl(A_COS + gc + th * 512, 512)
                            ssl = asl(A_SIN + gc + th * 512, 512)
                            P.op("vector", lambda e, b=bank, c=csl: e.tensor_mul(
                                    ropeA[:], b[:], c), deps=[f"qps_{g}_{oc}_{th}"])
                            P.op("vector", lambda e, b=bank, s2=ssl: e.tensor_mul(
                                    ropeB[0:64, :], b[64:128, :], s2[0:64, :]), deps=[])
                            ev = f"rpb_{g}_{oc}_{th}"
                            P.op("vector", lambda e, b=bank, s2=ssl: e.tensor_mul(
                                    ropeB[64:128, :], b[0:64, :], s2[64:128, :]),
                                 deps=[], sig=ev)
                            qkv_bank_free[bi] = ev
                            P.op("vector", lambda e, d=dst: e.tensor_sub(
                                    d[0:64, :], ropeA[0:64, :], ropeB[0:64, :]), deps=[])
                            P.op("vector", lambda e, d=dst: e.tensor_add(
                                    d[64:128, :], ropeA[64:128, :], ropeB[64:128, :]),
                                 deps=[], sig=f"rope_{g}_{oc}_{th}")
                    wring_free[slot] = ((f"qkvlast_{g}" if th == 1 else f"vps_{g}_{th}_3")
                                        if oc == 17 else f"qps_{g}_{oc}_{th}")

            p1_sq(0, range(DC))
            p1_stats(0)
            p1_sq(1, range(4))
            p1_chain(0)
            p1_sq(1, range(4, DC))
            p1_stats(1)
            p1_chain(1)
            p1_rest(0)
            p1_rest(1)

        if debug:
            for nm, dst, off, n in (("tk", "d_k", A_K, T), ("tv", "d_v", A_V, T),
                                    ("tq", "d_q", A_Q, 32768), ("txn", "d_xn", A_XN, 16384)):
                dd = {"d_k": d_k, "d_v": d_v, "d_q": d_q, "d_xn": d_xn}[dst]
                P.op("sync", lambda e, dd=dd, off=off, n=n: e.dma_start(
                        out=dd, in_=asl(off, n)),
                     deps=P.snapshot_deps(), dma="dbp")

        P.phase_barrier()

        precast = ([(wo_p[i], wso[i], f"pco_{i}") for i in range(16)] +
                   [(wd_p[i], wsd[i], f"pcd_{i}") for i in range(3)] +
                   [(wg_p[i], wsg[i], f"pcg_{i}") for i in range(ICN)] +
                   [(wu_p[i], wsu[i], f"pcu_{i}") for i in range(ICN)] +
                   [(wd_p[i], wsd[i], f"pcd_{i}") for i in range(3, 16)])
        for j, (src, dst, lbl) in enumerate(precast):
            P.op("gpsimd",
                 lambda e, src=src, dst=dst: e.dma_start(out=dst, in_=src),
                 deps=[precast[j - 8][2]] if j >= 8 else ["exp_111"],
                 dma=f"pc{j % 8}", sig=lbl)

        sc_t = [scA, scB]
        expp = [asl(A_XN + i * 1024, 1024) for i in range(4)]
        avsb = [asl(A_AVSB + i * 512, 512) for i in range(2)]
        k2sb = asl(A_XN, 4096)
        rcpbf2 = [rcpbf[:, 0:512], rcpbf[:, 512:1024]]
        q2sb = [asl(A_XN + 4096 + i * 512, 512) for i in range(2)]
        qksb = [asl(A_XN + 5120 + i * 512, 512) for i in range(2)]

        P.op("scalar", lambda e: e.activation(k2sb, asl(A_K, T), AF.Square),
             deps=[], sig="p2_k2")
        for c in range(8):
            P.op("tensor", lambda e, c=c: e.matmul(
                    smb[:], ones128[:], k2sb[:, c * 512:c * 512 + 512],
                    start=True, stop=True),
                 deps=[f"p2_k2", f"p2_kred_{c - 1}"], sig=f"p2_kmm_{c}")
            P.op("vector", lambda e, c=c: e.reduce_max(
                    mkrow[:, c:c + 1], smb[0:1, :], mybir.AxisListType.X),
                 deps=[f"p2_kmm_{c}"], sig=f"p2_kred_{c}")
        P.op("vector", lambda e: e.reduce_max(
                mkrow[:, 8:9], mkrow[:, 0:8], mybir.AxisListType.X),
             deps=[], sig="p2_mk2")
        P.op("scalar", lambda e: e.activation(
                mkrow[:, 10:11], mkrow[:, 8:9], AF.Ln), deps=["p2_mk2"])
        P.op("scalar", lambda e: e.activation(
                mkrow[:, 9:10], mkrow[:, 10:11], AF.Exp, scale=0.5),
             deps=[], sig="p2_mk")

        def p2a_rows(gh):
            P.op("scalar", lambda e: e.activation(
                    rows[:, 0:512], smb[0:1, :], AF.Ln),
                 deps=[f"p2_nrm_{gh}"], sig=f"p2_ln_{gh}")
            P.op("scalar", lambda e: e.activation(
                    rows[:, 512:1024], rows[:, 0:512], AF.Exp, scale=0.5),
                 deps=[])
            P.op("scalar", lambda e: e.activation(
                    rows[:, 1024:1536], rows[:, 512:1024], AF.Copy,
                    scale=mkrow[:, 9:10], bias=-800.0),
                 deps=[f"p2_maxc_{gh - 1}", "p2_mk"], sig=f"p2_t1_{gh}")
            P.op("vector", lambda e: e.tensor_max(
                    rows[:, 1536:2048], rows[:, 1024:1536], rcb[0:1, :]),
                 deps=[f"p2_t1_{gh}", f"p2_self_{gh}"], sig=f"p2_maxc_{gh}")
            P.op("vector", lambda e, gh=gh: e.tensor_scalar_mul(
                    negcbf[:, (gh % 2) * 512:(gh % 2) * 512 + 512],
                    rows[:, 1536:2048], -1.0),
                 deps=[f"p2_cw_{gh - 2}"], sig=f"p2_negc_{gh}")
            P.op("sync", lambda e, gh=gh: e.dma_start(
                    out=cdram[gh:gh + 1, :],
                    in_=negcbf[:, (gh % 2) * 512:(gh % 2) * 512 + 512]),
                 deps=[f"p2_negc_{gh}"], dma=f"cs{gh % 2}", sig=f"p2_cw_{gh}")

        for gh in range(NQT * NH):
            qt_, h_ = divmod(gh, NH)
            qap = asl(A_Q + h_ * TQ + qt_ * 512, 512)
            kslf = asl(A_K + qt_ * 512, 512)
            P.op("vector", lambda e, q=qap, k=kslf, gh=gh: e.tensor_mul(
                    qksb[gh % 2], q, k),
                 deps=[f"p2_self_{gh - 2}"], sig=f"p2_qk_{gh}")
            P.op("vector", lambda e, q=qap, gh=gh: e.tensor_mul(
                    q2sb[gh % 2], q, q),
                 deps=[f"p2_nrm_{gh - 2}"], sig=f"p2_q2_{gh}")
            P.op("tensor", lambda e, gh=gh: e.matmul(
                    smb[:], ones128[:], q2sb[gh % 2],
                    start=True, stop=True),
                 deps=[f"p2_q2_{gh}", f"p2_ln_{gh - 1}",
                       f"p2_kred_{7}" if gh == 0 else None],
                 sig=f"p2_nrm_{gh}")
            P.op("tensor", lambda e, gh=gh: e.matmul(
                    rcb[:], ones128[:], qksb[gh % 2],
                    start=True, stop=True),
                 deps=[f"p2_qk_{gh}", f"p2_maxc_{gh - 1}"],
                 sig=f"p2_self_{gh}")
            if gh > 0:
                p2a_rows(gh - 1)
        p2a_rows(NQT * NH - 1)

        acc_s = [h1r[:, 0:1024], h1r[:, 1024:2048]]

        def emit_avsum(gh, pr, avb):
            gpr = gh * 16 + pr
            P.op("scalar",
                 lambda e, pr=pr: e.activation(
                     expp[pr % 4], sc_t[pr % 2][:], AF.Exp, scale=EXP_SCALE),
                 deps=[f"sc_{gpr}", f"avp_{gpr - 4}", f"sma_{gpr - 4}"],
                 sig=f"exp_{gpr}")
            for half in range(2):
                tkc = 2 * pr + half
                P.op("tensor",
                     lambda e, tkc=tkc, pr=pr, half=half, b=avb: e.matmul(
                         b[:], asl(A_V + tkc * 128, 128),
                         expp[pr % 4][:, half * 512:half * 512 + 512],
                         start=(pr == 0 and half == 0),
                         stop=(pr == 15 and half == 1)),
                     deps=[f"exp_{gpr}"] if half == 0 else [],
                     sig=(([f"avp_{gpr}", f"av_{gh}"] if pr == 15 else
                           f"avp_{gpr}") if half == 1 else None))
            if pr == 0:
                P.op("vector", lambda e, gh=gh, pr=pr: e.tensor_copy(
                        acc_s[gh % 2], expp[pr % 4]),
                     deps=[f"exp_{gpr}", f"fold_{gh - 2}"], sig=f"sma_{gpr}")
            else:
                P.op("vector", lambda e, gh=gh, pr=pr: e.tensor_add(
                        acc_s[gh % 2], acc_s[gh % 2], expp[pr % 4]),
                     deps=[f"exp_{gpr}"], sig=f"sma_{gpr}")

        def emit_smm(gh):
            P.op("tensor", lambda e, gh=gh: e.matmul(
                    smb[:], ones128[:], q2sb[gh % 2], start=True, stop=True),
                 deps=[f"fold_{gh}", f"rcp_{gh - 1}"] +
                      ([f"p2_ln_{NQT * NH - 1}"] if gh == 0 else []),
                 sig=f"smm_{gh}")

        def emit_rcp(gh):
            P.op("scalar", lambda e: e.activation(
                    rows[:, 0:512], smb[0:1, :], AF.Ln),
                 deps=[f"smm_{gh}"] + ([f"dbgr_{gh - 1}"] if debug else []))
            P.op("scalar", lambda e, gh=gh: e.activation(
                    rcpbf2[gh % 2], rows[:, 0:512], AF.Exp, scale=-1.0),
                 deps=[f"rec_{gh - 2}"], sig=f"rcp_{gh}")

        def emit_fold(gh):
            P.op("vector", lambda e, gh=gh: e.tensor_add(
                    q2sb[gh % 2], acc_s[gh % 2][:, 0:512],
                    acc_s[gh % 2][:, 512:1024]),
                 deps=[f"sma_{gh * 16 + 15}", f"smm_{gh - 2}"],
                 sig=f"fold_{gh}")

        def emit_avcp(gh):
            P.op("vector", lambda e, b=[av0, av1][gh % 2], gh=gh:
                 e.tensor_copy(avsb[gh % 2], b[:]),
                 deps=[f"av_{gh}", f"att_{gh - 2}"] +
                      ([f"dbga_{gh - 2}"] if debug else []),
                 sig=f"avcp_{gh}")

        def emit_recatt(gh):
            qt_, h_ = divmod(gh, NH)
            qap = asl(A_Q + h_ * TQ + qt_ * 512, 512)
            P.op("tensor", lambda e, gh=gh: e.matmul(
                    rcb[:], oner[:], rcpbf2[gh % 2], start=True, stop=True),
                 deps=[f"rcp_{gh}", f"att_{gh - 1}"], sig=f"rec_{gh}")
            P.op("vector", lambda e, q=qap, gh=gh: e.tensor_mul(
                    q, avsb[gh % 2], rcb[:]),
                 deps=[f"rec_{gh}", f"avcp_{gh}"] +
                      ([f"dbgq_{gh}"] if debug else []),
                 sig=f"att_{gh}")

        ncbc = [asl(A_NEGC + i * 512, 512) for i in range(2)]

        def emit_ncload(gh):
            if gh >= NQT * NH:
                return
            P.op("sync", lambda e, gh=gh: e.dma_start(
                    out=ncbc[gh % 2],
                    in_=bass.AP(cdram.tensor, gh * 512, [[0, 128], [1, 512]])),
                 deps=[f"p2_cw_{gh}", f"sc_{(gh - 2) * 16 + 15}"],
                 dma=f"cl{gh % 2}", sig=f"p2_cl_{gh}")
        emit_ncload(0)
        for qt in range(NQT):
            for h in range(NH):
                gh = qt * NH + h
                qap = asl(A_Q + h * TQ + qt * 512, 512)
                avb = [av0, av1][gh % 2]
                emit_ncload(gh + 1)
                if debug:
                    P.op("sync", lambda e, q=qap, gh=gh: e.dma_start(
                            out=d_qrd[gh * 128:(gh + 1) * 128, :], in_=q),
                         deps=[f"p2_cl_{gh}"], dma="dbq", sig=f"dbgq_{gh}")
                for pr in range(16):
                    gpr = gh * 16 + pr
                    if gh > 0:
                        if pr == 0:
                            emit_avsum(gh - 1, 14, [av0, av1][(gh - 1) % 2])
                        if pr == 1:
                            emit_avsum(gh - 1, 15, [av0, av1][(gh - 1) % 2])
                            emit_fold(gh - 1)
                        if pr == 2:
                            emit_avcp(gh - 1)
                        if pr == 4:
                            emit_smm(gh - 1)
                        if pr == 6:
                            emit_rcp(gh - 1)
                    P.op("tensor",
                         lambda e, pr=pr, gh=gh: e.matmul(
                             sc_t[pr % 2][:, 0:512], ident[:], ncbc[gh % 2],
                             start=True, stop=False),
                         deps=[f"exp_{gpr - 2}"] +
                              ([f"p2_cl_{gh}", f"avcp_{gh - 2}"]
                               if pr == 0 else []))
                    P.op("tensor",
                         lambda e, pr=pr, q=qap, tkc=2 * pr: e.matmul(
                             sc_t[pr % 2][:, 0:512],
                             asl(A_K + tkc * 128, 128), q,
                             start=False, stop=True),
                         deps=[])
                    P.op("tensor",
                         lambda e, pr=pr, gh=gh: e.matmul(
                             sc_t[pr % 2][:, 512:1024], ident[:], ncbc[gh % 2],
                             start=True, stop=False),
                         deps=[])
                    P.op("tensor",
                         lambda e, pr=pr, q=qap, tkc=2 * pr + 1: e.matmul(
                             sc_t[pr % 2][:, 512:1024],
                             asl(A_K + tkc * 128, 128), q,
                             start=False, stop=True),
                         deps=[], sig=f"sc_{gpr}")
                    if pr >= 2:
                        emit_avsum(gh, pr - 2, avb)
                if debug and gh > 0:
                    P.op("sync", lambda e, gh=gh - 1: e.dma_start(
                            out=d_rcp[gh:gh + 1, :], in_=rows[:, 0:512]),
                         deps=[f"rcp_{gh}"], dma="dbr", sig=f"dbgr_{gh}")
                    P.op("sync", lambda e, gh=gh - 1: e.dma_start(
                            out=d_av[gh * 128:(gh + 1) * 128, :], in_=avsb[gh % 2]),
                         deps=[f"avcp_{gh}"], dma="dba", sig=f"dbga_{gh}")
                if gh > 0:
                    emit_recatt(gh - 1)
        gl = NQT * NH - 1
        emit_avsum(gl, 14, [av0, av1][gl % 2])
        emit_avsum(gl, 15, [av0, av1][gl % 2])
        emit_fold(gl)
        emit_avcp(gl)
        emit_smm(gl)
        emit_rcp(gl)
        emit_recatt(gl)

        if debug:
            P.op("sync", lambda e: e.dma_start(out=d_att, in_=asl(A_Q, 32768)),
                 deps=P.snapshot_deps(), dma="dbp")

        P.phase_barrier()
        h1bf_s = [asl(A_H1BF + i * 512, 512) for i in range(2)]
        h1sq_s = [asl(A_H1SQ + i * 512, 512) for i in range(2)]
        def emit_h1n(fg):
            fcol = fg * 512
            P.op("tensor", lambda e, fg=fg: e.matmul(
                    scB[:, 0:512], oner[:],
                    ln2bf[:, fg * 512:fg * 512 + 512], start=True, stop=True),
                 deps=[f"ln2_{fg}", f"fact_{fg - 1}_{ICN - 1}"],
                 sig=f"bc2m_{fg}")
            P.op("tensor", lambda e, fg=fg: e.matmul(
                    scB[:, 512:1024], oner[:],
                    ln2bf[:, 2048 + fg * 512:2048 + fg * 512 + 512],
                    start=True, stop=True),
                 deps=[], sig=f"bc2r_{fg}")
            for dc in range(DC):
                P.op("sync" if fg == 0 else "gpsimd",
                     lambda e, dc=dc, fcol=fcol: e.dma_start(
                         out=h1r[:, (dc % 4) * 512:(dc % 4) * 512 + 512],
                         in_=h1T[dc * 128:dc * 128 + 128, fcol:fcol + 512]),
                     deps=[f"h1w_{fg * 16 + dc}",
                           f"h1n_{fg}_{dc - 4}" if dc >= 4
                           else f"h1n_{fg - 1}_{12 + dc}"],
                     dma=(f"h0r{dc % 4}" if fg == 0 else f"hr{dc % 4}"),
                     sig=f"h1rl_{fg}_{dc}")
                P.op("vector", lambda e, dc=dc: e.tensor_sub(
                        ropeA[:], h1r[:, (dc % 4) * 512:(dc % 4) * 512 + 512],
                        scB[:, 0:512]),
                     deps=[f"h1rl_{fg}_{dc}", f"bc2m_{fg}",
                           f"ups_{fg - 1}_{ICN - 1}" if dc == 0 else None])
                P.op("vector", lambda e, dc=dc: e.tensor_mul(
                        asl(A_H1N + dc * 512, 512), ropeA[:],
                        scB[:, 512:1024]),
                     deps=[f"bc2r_{fg}"], sig=f"h1n_{fg}_{dc}")

        def emit_p3stats(tt, oc):
            k = tt * 16 + oc
            lx = f"sx2_{k}"
            lq = f"sq2_{k}"
            P.op("tensor",
                 lambda e, k=k, oc=oc: e.matmul(
                     scA[:, 0:512], ones128[:], h1bf_s[k % 2],
                     start=(oc == 0), stop=(oc == 15)),
                 deps=[f"h1b_{k}"] + ([f"rowm2_{tt - 1}"] if oc == 0 else []),
                 sig=([lx, f"stx2_{tt}"] if oc == 15 else lx))
            P.op("tensor",
                 lambda e, k=k, oc=oc: e.matmul(
                     scA[:, 512:1024], ones128[:], h1sq_s[k % 2],
                     start=(oc == 0), stop=(oc == 15)),
                 deps=[f"h1q_{k}"] + ([f"rowv2_{tt - 1}"] if oc == 0 else []),
                 sig=([lq, f"stq2_{tt}"] if oc == 15 else lq))

        for tt in range(NTT):
            for oc in range(16):
                k = tt * 16 + oc
                slot = wload(wso[oc], f"wo_{k}",
                             extra=([f"pco_{oc}"] if tt == 0 else []))
                bank = [av0, av1][oc % 2]
                P.op("sync",
                     lambda e, oc=oc, tt=tt: e.dma_start(
                         out=xres[:, (oc % 2) * 512:(oc % 2) * 512 + 512],
                         in_=xt[oc * 128:oc * 128 + 128,
                                tt * 512:tt * 512 + 512]),
                     deps=[f"h1a_{k - 2}"], dma=f"xr{k % 2}", sig=f"xrl_{k}")
                for dc in range(DC):
                    P.op("tensor",
                         lambda e, dc=dc, tt=tt, b=bank, s=slot: e.matmul(
                             b[:], wtile(s, dc),
                             asl(A_Q + dc * TQ + tt * 512, 512),
                             start=(dc == 0), stop=(dc == DC - 1)),
                         deps=([f"wo_{k}", f"h1a_{k - 2}"] if dc == 0 else []),
                         sig=(f"wps_{k}" if dc == DC - 1 else None))
                wring_free[slot] = f"wps_{k}"
                P.op("vector",
                     lambda e, k=k, b=bank: e.tensor_add(
                         h1sb[:, (k % 3) * 512:(k % 3) * 512 + 512], b[:],
                         xres[:, (k % 2) * 512:(k % 2) * 512 + 512]),
                     deps=[f"wps_{k}", f"xrl_{k}", f"h1w_{k - 3}",
                           f"h1b_{k - 3}"],
                     sig=f"h1a_{k}")
                P.op("scalar",
                     lambda e, k=k: e.activation(
                         h1sq_s[k % 2],
                         h1sb[:, (k % 3) * 512:(k % 3) * 512 + 512], AF.Square),
                     deps=[f"h1a_{k}", f"sq2_{k - 2}" if k >= 2 else None],
                     sig=f"h1q_{k}")
                P.op("scalar",
                     lambda e, k=k: e.activation(
                         h1bf_s[k % 2],
                         h1sb[:, (k % 3) * 512:(k % 3) * 512 + 512], AF.Copy),
                     deps=[f"sx2_{k - 2}" if k >= 2 else None],
                     sig=f"h1b_{k}")
                P.op("sync",
                     lambda e, k=k, oc=oc, tt=tt: e.dma_start(
                         out=h1T[oc * 128:oc * 128 + 128,
                                 tt * 512:tt * 512 + 512],
                         in_=h1sb[:, (k % 3) * 512:(k % 3) * 512 + 512]),
                     deps=[f"h1a_{k}"], dma=f"hs{k % 3}", sig=f"h1w_{k}")
                if oc > 0:
                    emit_p3stats(tt, oc - 1)
                if oc == 15:
                    emit_p3stats(tt, 15)
            P.op("scalar", lambda e: e.activation(
                    rows[:, 0:512], scA[0:1, 0:512], AF.Copy, scale=INV_D),
                 deps=[f"stx2_{tt}"], sig=f"rowm2_{tt}")
            P.op("scalar", lambda e: e.activation(
                    rows[:, 512:1024], rows[:, 0:512], AF.Square), deps=[],
                 sig=f"msq2_{tt}")
            P.op("vector", lambda e: e.scalar_tensor_tensor(
                    rows[:, 1024:1536], scA[0:1, 512:1024], INV_D,
                    rows[:, 512:1024], ALU.mult, ALU.subtract),
                 deps=[f"stq2_{tt}", f"msq2_{tt}"], sig=f"rowv2_{tt}")
            P.op("scalar", lambda e: e.activation(
                    rows[:, 1536:2048], rows[:, 1024:1536], AF.Ln, bias=EPS),
                 deps=[f"rowv2_{tt}"])
            P.op("scalar", lambda e: e.activation(
                    rows[:, 512:1024], rows[:, 1536:2048], AF.Exp, scale=-0.5),
                 deps=[])
            P.op("scalar", lambda e, tt=tt: e.activation(
                    ln2bf[:, tt * 512:tt * 512 + 512], rows[:, 0:512], AF.Copy),
                 deps=[])
            P.op("scalar", lambda e, tt=tt: e.activation(
                    ln2bf[:, 2048 + tt * 512:2048 + tt * 512 + 512],
                    rows[:, 512:1024], AF.Copy),
                 deps=[], sig=f"ln2_{tt}")
            if tt == 0:
                emit_h1n(0)

        P.phase_barrier()
        silu_s = [asl(A_AVSB + i * 512, 512) for i in range(2)]
        wd_free = [None, None, None]

        for fg in range(NFG):
            fcol = fg * 512
            for ic in range(ICN):
                sg = wload(wsg[ic], f"wg_{fg}_{ic}",
                           extra=([f"pcg_{ic}"] if fg == 0 else []))
                su = wload(wsu[ic], f"wu_{fg}_{ic}",
                           extra=([f"pcu_{ic}"] if fg == 0 else []))
                if ic == 1:
                    for oc2 in range(3):
                        wdfr0 = wd_free[oc2 % 3]
                        wdfr0 = wdfr0 if isinstance(wdfr0, list) else [wdfr0]
                        if fg == 0:
                            wdfr0 = wdfr0 + [f"pcd_{oc2}"]
                        P.op("gpsimd",
                             lambda e, oc=oc2: e.dma_start(
                                 out=asl(WD_SLOTS[oc % 3], INTER),
                                 in_=wsd[oc]),
                             deps=wdfr0, dma=f"wd{oc2 % 3}",
                             sig=f"wdl_{fg}_{oc2}")
                gb = scA[:, (ic % 2) * 512:(ic % 2) * 512 + 512]
                ub = scB[:, (ic % 2) * 512:(ic % 2) * 512 + 512]
                for dc in range(DC):
                    P.op("tensor",
                         lambda e, dc=dc, s=sg, b=gb: e.matmul(
                             b, wtile(s, dc), asl(A_H1N + dc * 512, 512),
                             start=(dc == 0), stop=(dc == DC - 1)),
                         deps=([f"wg_{fg}_{ic}",
                                f"silu_{fg}_{ic - 2}" if ic >= 2
                                else f"silu_{fg - 1}_{ICN - 2 + ic}"]
                               if dc == 0 else []) +
                              ([f"h1n_{fg}_{dc}"] if ic == 0 else []),
                         sig=(f"gps_{fg}_{ic}" if dc == DC - 1 else None))
                for dc in range(DC):
                    P.op("tensor",
                         lambda e, dc=dc, s=su, b=ub: e.matmul(
                             b, wtile(s, dc), asl(A_H1N + dc * 512, 512),
                             start=(dc == 0), stop=(dc == DC - 1)),
                         deps=([f"wu_{fg}_{ic}", f"h1n_{fg}_{DC - 1}",
                                f"fact_{fg}_{ic - 2}" if ic >= 2 else None]
                               if dc == 0 else []),
                         sig=(f"ups_{fg}_{ic}" if dc == DC - 1 else None))
                wring_free[sg] = f"gps_{fg}_{ic}"
                wring_free[su] = f"ups_{fg}_{ic}"
                P.op("scalar",
                     lambda e, ic=ic, b=gb: e.activation(
                         silu_s[ic % 2], b, AF.Silu, scale=gs[:, ic:ic + 1]),
                     deps=[f"gps_{fg}_{ic}"], sig=f"silu_{fg}_{ic}")
                P.op("vector",
                     lambda e, ic=ic, b=ub: e.scalar_tensor_tensor(
                         asl(A_ACT + ic * 512, 512), b, us[:, ic:ic + 1],
                         silu_s[ic % 2], ALU.mult, ALU.mult),
                     deps=[f"ups_{fg}_{ic}", f"silu_{fg}_{ic}",
                           f"fps_{fg - 1}_15" if ic == 0 else None],
                     sig=f"fact_{fg}_{ic}")
            if fg + 1 < NFG:
                emit_h1n(fg + 1)
            for oc in range(16):
                ko = fg * 16 + oc
                if oc >= 3:
                    wdfr = wd_free[oc % 3]
                    wdfr = wdfr if isinstance(wdfr, list) else [wdfr]
                    if fg == 0:
                        wdfr = wdfr + [f"pcd_{oc}"]
                    P.op("gpsimd",
                         lambda e, oc=oc: e.dma_start(
                             out=asl(WD_SLOTS[oc % 3], INTER),
                             in_=wsd[oc]),
                         deps=wdfr, dma=f"wd{oc % 3}",
                         sig=f"wdl_{fg}_{oc}")
                bank = [av0, av1][oc % 2]
                for ic in range(ICN):
                    P.op("tensor",
                         lambda e, ic=ic, oc=oc, b=bank: e.matmul(
                             b[:], asl(WD_SLOTS[oc % 3] + ic * 128, 128),
                             asl(A_ACT + ic * 512, 512),
                             start=(ic == 0), stop=(ic == ICN - 1)),
                         deps=([f"wdl_{fg}_{oc}", f"fact_{fg}_{ICN - 1}",
                                f"yv_{ko - 2}"] if ic == 0 else []),
                         sig=(f"fps_{fg}_{oc}" if ic == ICN - 1 else None))
                wd_free[oc % 3] = f"fps_{fg}_{oc}"
                P.op("sync",
                     lambda e, oc=oc, fcol=fcol: e.dma_start(
                         out=h1r[:, 2048 + (oc % 2) * 512:
                                 2048 + (oc % 2) * 512 + 512],
                         in_=h1T[oc * 128:oc * 128 + 128, fcol:fcol + 512]),
                     deps=[f"yv_{ko - 2}"], dma=f"h2{ko % 2}", sig=f"h1rl2_{ko}")
                P.op("vector",
                     lambda e, oc=oc, b=bank: e.scalar_tensor_tensor(
                         ysb[:, (oc % 2) * 512:(oc % 2) * 512 + 512], b[:],
                         ds[:, oc:oc + 1],
                         h1r[:, 2048 + (oc % 2) * 512:2048 + (oc % 2) * 512 + 512],
                         ALU.mult, ALU.add),
                     deps=[f"fps_{fg}_{oc}", f"h1rl2_{ko}", f"yw_{ko - 2}"],
                     sig=f"yv_{ko}")
                P.op("sync",
                     lambda e, oc=oc, fcol=fcol: e.dma_start(
                         out=yt[oc * 128:oc * 128 + 128, fcol:fcol + 512],
                         in_=ysb[:, (oc % 2) * 512:(oc % 2) * 512 + 512]),
                     deps=[f"yv_{ko}"], dma=f"ys{ko % 2}", sig=f"yw_{ko}")

        P.op("sync", lambda e: e.nop(), deps=P.snapshot_deps())

        FIN = P.finalize()
        sems = {s: E(nc.semaphore(s)) for s in P.counts}
        global LAST_PLAN
        LAST_PLAN = P
        with nc.Block() as block:
            def mk(engname):
                def body(eng):
                    for waits, fn, inc in FIN[engname]:
                        for s, v in waits:
                            eng.wait_ge(sems[s], v)
                        inst = fn(eng)
                        if inc is not None:
                            sname, amt = inc
                            inst.then_inc(sems[sname], amt)
                return body
            block.tensor(mk("tensor"))
            block.vector(mk("vector"))
            block.scalar(mk("scalar"))
            block.sync(mk("sync"))
            block.gpsimd(mk("gpsimd"))

    return nc



def _prep_w(W):
    O, I = W.shape
    return np.ascontiguousarray(
        W.reshape(O // 128, 128, I // 128, 128).transpose(0, 3, 2, 1)
    ).reshape(O // 128, 128, I)


_CACHED = {}


def kernel(L0_attn_x, L0_attn_w_attn, L0_attn_w_out, L0_ffn_gate_w,
           L0_ffn_up_w, L0_ffn_down_w, L0_ffn_gate_s, L0_ffn_up_s,
           L0_ffn_down_s):
    x = np.asarray(L0_attn_x)
    B = x.shape[0]
    bf = ml_dtypes.bfloat16

    pos = (10000.0 ** ((-2.0 * np.arange(0, HD, 2) - 1.0) / HD)).astype(np.float32)
    ang = np.arange(T, dtype=np.float32)[:, None] * pos[None, :]
    ang = np.concatenate([ang, ang], axis=1)
    cosT = np.ascontiguousarray(np.cos(ang).T)
    sinT = np.ascontiguousarray(np.sin(ang).T)

    wq = _prep_w(np.asarray(L0_attn_w_attn))
    wo = _prep_w(np.asarray(L0_attn_w_out))
    wg = _prep_w(np.asarray(L0_ffn_gate_w))
    wu = _prep_w(np.asarray(L0_ffn_up_w))
    wd = _prep_w(np.asarray(L0_ffn_down_w))
    gsv = np.ascontiguousarray(np.asarray(L0_ffn_gate_s).reshape(ICN, 128).T)
    usv = np.ascontiguousarray(np.asarray(L0_ffn_up_s).reshape(ICN, 128).T)
    dsv = np.ascontiguousarray(np.asarray(L0_ffn_down_s).reshape(16, 128).T)
    cval = np.tile(np.array([CONSTS], np.float32), (128, 1))
    onec = np.ones((128, 1), bf)
    oner = np.ones((1, 128), bf)
    ones128 = np.ones((128, 128), bf)
    ident = np.eye(128).astype(bf)

    nc = build_program()

    in_maps = []
    for c in range(8):
        b, hh = c // 2, c % 2
        if hh == 0:
            xtc = np.ascontiguousarray(x[b].T)
            cosc, sinc = cosT, sinT
        else:
            xtc = np.ascontiguousarray(np.concatenate(
                [x[b, 2048:4096].T, x[b, 0:2048].T], axis=1))
            cosc = np.ascontiguousarray(np.concatenate(
                [cosT[:, 2048:4096], cosT[:, 0:2048]], axis=1))
            sinc = np.ascontiguousarray(np.concatenate(
                [sinT[:, 2048:4096], sinT[:, 0:2048]], axis=1))
        in_maps.append(dict(
            xt=xtc, cosb=cosc.astype(bf), sinb=sinc.astype(bf),
            wq_p=wq, wo_p=wo, wg_p=wg, wu_p=wu, wd_p=wd,
            gs_d=gsv, us_d=usv, ds_d=dsv, cvals=cval,
            onec_d=onec, oner_d=oner, ones128_d=ones128, ident_d=ident))

    res = run_bass_kernel_spmd(nc, in_maps, core_ids=list(range(8)))

    out = np.empty((B, T, D), np.float32)
    for c in range(8):
        b, hh = c // 2, c % 2
        out[b, hh * 2048:(hh + 1) * 2048, :] = res.results[c]["yt"].T
    return out


# version 30
# speedup vs baseline: 1.0846x; 1.0846x over previous
import numpy as np
import ml_dtypes
import concourse.bass as bass
import concourse.mybir as mybir
from concourse.bass_utils import run_bass_kernel_spmd

F32 = mybir.dt.float32
BF16 = mybir.dt.bfloat16
AF = mybir.ActivationFunctionType
ALU = mybir.AluOpType

D = 2048
NH = 16
HD = 128
T = 4096
TQ = 2048
INTER = 5632
DC = D // 128
ICN = INTER // 128
G_P1 = 4
TG1 = 1024
NQT = 4
NTT = 4
NFG = 4
EXP_SCALE = 1.0 / float(np.sqrt(128.0))
EXP_BIAS = -12.0
EPS = 1e-5
INV_D = 1.0 / 2048.0

A_COS = 0
A_SIN = 4096
A_K = 8192
A_V = 12288
A_Q = 16384
A_XN = 49152
A_H1N = 0
A_ACT = 16384
WD_SLOTS = [49152, 54784, 38912]
A_WRING = 65536
A_EXPP = 73728
A_H1BF = 75776
A_H1SQ = 76800
A_AVSB = 77824
A_NEGC = 75776
ARENA_N = 78848

CONSTS = [0.0, 1.0, INV_D, -0.5, -1.0, EPS, EXP_BIAS, EXP_SCALE, -800.0, 0.5]

LAST_PLAN = None
ENGS = ("tensor", "vector", "scalar", "sync", "gpsimd")
ESEM = {"tensor": "pe", "vector": "dve", "scalar": "act", "sync": "sy",
        "gpsimd": "ld"}
COMPUTE_SEMS = ("pe", "dve", "act")
ALL_SEMS = ("pe", "dve", "act", "sy", "ld", "st")


class Plan:
    def __init__(self):
        self.ops = {e: [] for e in ENGS}
        self.counts = {s: 0 for s in ("pe", "dve", "act")}
        self.events = {}
        self.missing = []

    def op(self, eng, fn, deps=(), sig=None, dma=None):
        inc = None
        if sig is not None or dma is not None:
            if dma is not None:
                sname, amt = dma, 16
            else:
                sname, amt = ESEM[eng], 1
            self.counts[sname] = self.counts.get(sname, 0) + amt
            inc = (sname, amt)
            if sig is not None:
                labels = sig if isinstance(sig, (list, tuple)) else [sig]
                for lb in labels:
                    assert lb not in self.events, f"duplicate label {lb}"
                    self.events[lb] = (sname, self.counts[sname])
        self.ops[eng].append(([d for d in deps if d is not None], fn, inc))

    def snapshot_deps(self):
        out = []
        for s in list(self.counts):
            lbl = f"_bar_{s}_{len(self.ops['sync'])}_{len(self.ops['tensor'])}_{self.counts[s]}"
            self.events[lbl] = (s, self.counts[s])
            out.append(lbl)
        return out

    def phase_barrier(self):
        deps = self.snapshot_deps()
        for e in ("tensor", "vector", "scalar"):
            self.op(e, lambda eng: eng.nop(), deps=deps)

    def finalize(self):
        out = {}
        for eng in ENGS:
            last = {}
            lst = []
            for deps, fn, inc in self.ops[eng]:
                waits = []
                for d in deps:
                    if d not in self.events:
                        self.missing.append((eng, d))
                        continue
                    s, v = self.events[d]
                    if s in COMPUTE_SEMS and s == ESEM[eng]:
                        continue
                    if last.get(s, -1) < v:
                        waits.append((s, v))
                        last[s] = v
                lst.append((waits, fn, inc))
            out[eng] = lst
        return out


def build_program(debug=False, rc_detect=True):
    nc = bass.Bass("TRN2", target_bir_lowering=False,
                   detect_race_conditions=rc_detect)

    xt = nc.dram_tensor("xt", [D, T], F32, kind="ExternalInput").ap()
    cosb = nc.dram_tensor("cosb", [HD, T], BF16, kind="ExternalInput").ap()
    sinb = nc.dram_tensor("sinb", [HD, T], BF16, kind="ExternalInput").ap()
    wq_p = nc.dram_tensor("wq_p", [18, 128, D], F32, kind="ExternalInput").ap()
    wo_p = nc.dram_tensor("wo_p", [16, 128, D], F32, kind="ExternalInput").ap()
    wg_p = nc.dram_tensor("wg_p", [ICN, 128, D], F32, kind="ExternalInput").ap()
    wu_p = nc.dram_tensor("wu_p", [ICN, 128, D], F32, kind="ExternalInput").ap()
    wd_p = nc.dram_tensor("wd_p", [16, 128, INTER], F32, kind="ExternalInput").ap()
    gs_d = nc.dram_tensor("gs_d", [128, ICN], F32, kind="ExternalInput").ap()
    us_d = nc.dram_tensor("us_d", [128, ICN], F32, kind="ExternalInput").ap()
    ds_d = nc.dram_tensor("ds_d", [128, 16], F32, kind="ExternalInput").ap()
    cvals = nc.dram_tensor("cvals", [128, len(CONSTS)], F32, kind="ExternalInput").ap()
    onec_d = nc.dram_tensor("onec_d", [128, 1], BF16, kind="ExternalInput").ap()
    ones128_d = nc.dram_tensor("ones128_d", [128, 128], BF16, kind="ExternalInput").ap()
    ident_d = nc.dram_tensor("ident_d", [128, 128], BF16, kind="ExternalInput").ap()
    oner_d = nc.dram_tensor("oner_d", [1, 128], BF16, kind="ExternalInput").ap()

    yt = nc.dram_tensor("yt", [D, TQ], F32, kind="ExternalOutput").ap()
    cdram = nc.dram_tensor("cdram", [64, 512], BF16, kind="Internal").ap()
    wso = nc.dram_tensor("wso", [16, 128, D], BF16, kind="Internal").ap()
    wsg = nc.dram_tensor("wsg", [ICN, 128, D], BF16, kind="Internal").ap()
    wsu = nc.dram_tensor("wsu", [ICN, 128, D], BF16, kind="Internal").ap()
    wsd = nc.dram_tensor("wsd", [16, 128, INTER], BF16, kind="Internal").ap()
    h1T = nc.dram_tensor("h1T", [D, TQ], F32,
                         kind="ExternalOutput" if debug else "Internal").ap()
    if debug:
        d_qrd = nc.dram_tensor("d_qrd", [64 * 128, 512], BF16, kind="ExternalOutput").ap()
        d_rcp = nc.dram_tensor("d_rcp", [64, 512], F32, kind="ExternalOutput").ap()
        d_av = nc.dram_tensor("d_av", [64 * 128, 512], BF16, kind="ExternalOutput").ap()
        d_k = nc.dram_tensor("d_k", [128, T], BF16, kind="ExternalOutput").ap()
        d_v = nc.dram_tensor("d_v", [128, T], BF16, kind="ExternalOutput").ap()
        d_q = nc.dram_tensor("d_q", [128, 32768], BF16, kind="ExternalOutput").ap()
        d_xn = nc.dram_tensor("d_xn", [128, 16384], BF16, kind="ExternalOutput").ap()
        d_att = nc.dram_tensor("d_att", [128, 32768], BF16, kind="ExternalOutput").ap()

    from contextlib import ExitStack
    with ExitStack() as stk:
        E = stk.enter_context
        cpool = E(nc.sbuf_tensor("cpool", [128, len(CONSTS)], F32))
        for j, v in enumerate(CONSTS):
            nc.const_aps.register(F32, v, cpool[:, j:j + 1])
        arena = E(nc.sbuf_tensor("arena", [128, ARENA_N], BF16))
        gs = E(nc.sbuf_tensor("gs", [128, ICN], F32))
        us = E(nc.sbuf_tensor("us", [128, ICN], F32))
        ds = E(nc.sbuf_tensor("ds", [128, 16], F32))
        onec = E(nc.sbuf_tensor("onec", [128, 1], BF16))
        ones128 = E(nc.sbuf_tensor("ones128", [128, 128], BF16))
        ident = E(nc.sbuf_tensor("ident", [128, 128], BF16))
        oner = E(nc.sbuf_tensor("oner", [1, 128], BF16))
        xres = E(nc.sbuf_tensor("xres", [128, 1024], F32))
        h1sb = E(nc.sbuf_tensor("h1sb", [128, 1536], F32))
        h1r = E(nc.sbuf_tensor("h1r", [128, 3072], F32))
        ysb = E(nc.sbuf_tensor("ysb", [128, 1024], F32))
        ropeA = E(nc.sbuf_tensor("ropeA", [128, 512], F32))
        ropeB = E(nc.sbuf_tensor("ropeB", [128, 512], F32))
        rows = E(nc.sbuf_tensor("rows", [1, 2048], F32))
        rows_bf = E(nc.sbuf_tensor("rows_bf", [1, 1024], BF16))
        ln2bf = E(nc.sbuf_tensor("ln2bf", [1, 4096], BF16))
        rcpbf = E(nc.sbuf_tensor("rcpbf", [1, 1024], BF16))
        negcbf = E(nc.sbuf_tensor("negcbf", [1, 1024], BF16))
        mkrow = E(nc.sbuf_tensor("mkrow", [1, 16], F32))

        scA = E(nc.psum_tensor("scA", [128, 1024], F32))
        scB = E(nc.psum_tensor("scB", [128, 1024], F32))
        av0 = E(nc.psum_tensor("av0", [128, 512], F32))
        av1 = E(nc.psum_tensor("av1", [128, 512], F32))
        smb = E(nc.psum_tensor("smb", [128, 512], F32))
        rcb = E(nc.psum_tensor("rcb", [128, 512], F32))

        P = Plan()

        def asl(off, n):
            return arena[:, off:off + n]

        inits = [
            (asl(A_COS, T), cosb), (asl(A_SIN, T), sinb),
            (gs[:], gs_d), (us[:], us_d), (ds[:], ds_d),
            (cpool[:], cvals), (onec[:], onec_d), (oner[:], oner_d),
            (ones128[:], ones128_d), (ident[:], ident_d),
        ]
        for i, (o, ii) in enumerate(inits):
            P.op("sync", lambda e, o=o, ii=ii: e.dma_start(out=o, in_=ii),
                 dma="init", sig=(f"init_{i}" if i == len(inits) - 1 else None))
        INIT = [f"init_{len(inits) - 1}"]
        for e in ("tensor", "vector", "scalar"):
            P.op(e, lambda eng: eng.nop(), deps=INIT)

        wring_free = [None, None, None, None]
        wring_seq = [0]

        def wload(src_ap, sig, extra=()):
            slot = wring_seq[0] % 4
            wring_seq[0] += 1
            fr = wring_free[slot]
            fr = fr if isinstance(fr, list) else [fr]
            P.op("gpsimd",
                 lambda e, s=slot, a=src_ap: e.dma_start(
                     out=asl(A_WRING + s * 2048, 2048), in_=a),
                 deps=fr + list(extra), dma=f"wr{slot}", sig=sig)
            return slot

        def wtile(slot, dc):
            return asl(A_WRING + slot * 2048 + dc * 128, 128)

        qkv_banks = [av0, av1]
        qkv_bank_free = [None, None]
        xsq_slots = [asl(A_H1BF + i * 512, 512) for i in range(4)]

        def xg_t(dc, th, w=512):
            return asl(A_XN + dc * TG1 + th * 512, w)

        bank_i = [0]

        def next_bank():
            bank_i[0] ^= 1
            return qkv_banks[bank_i[0]], bank_i[0]

        for g in range(G_P1):
            gc = g * TG1
            own = g < 2
            for th2 in range(2):
                xg_out = bass.AP(arena, A_XN + th2 * 512,
                                 [[ARENA_N, 128], [TG1, DC], [1, 512]])
                xg_in = xt.rearrange("(c p) t -> p c t", p=128)[
                    :, :, gc + th2 * 512:gc + th2 * 512 + 512]
                P.op("gpsimd",
                     lambda e, o=xg_out, ii=xg_in: e.dma_start(out=o, in_=ii),
                     deps=[f"vps_{g - 1}_{th2}_3" if th2 == 0
                           else f"qkvlast_{g - 1}"], dma=f"xg{th2}",
                     sig=f"xg_{g}_{th2}")

            def sq_lbl(n):
                if n < 0:
                    return None
                gg, rr = divmod(n, 2 * DC)
                tt_, dd = divmod(rr, DC)
                return f"p1sq_{gg}_{tt_}_{dd}"

            def p1_sq(th, dcs):
                nsq = (g * 2 + th) * DC
                for dc in dcs:
                    P.op("scalar",
                         lambda e, dc=dc, th=th: e.activation(
                             xsq_slots[dc % 4], xg_t(dc, th), AF.Square),
                         deps=[f"xg_{g}_{th}", sq_lbl(nsq + dc - 4)],
                         sig=f"p1aq_{g}_{th}_{dc}")

            def p1_stats(th):
                for dc in range(DC):
                    P.op("tensor",
                         lambda e, dc=dc, th=th: e.matmul(
                             scA[:, 0:512], ones128[:], xg_t(dc, th),
                             start=(dc == 0), stop=(dc == DC - 1)),
                         deps=([f"xg_{g}_{th}",
                                f"rowm_{g}_{th - 1}" if th else f"rowm_{g - 1}_1"]
                               if dc == 0 else []),
                         sig=(f"stx_{g}_{th}" if dc == DC - 1 else None))
                for dc in range(DC):
                    P.op("tensor",
                         lambda e, dc=dc, th=th: e.matmul(
                             scA[:, 512:1024], ones128[:], xsq_slots[dc % 4],
                             start=(dc == 0), stop=(dc == DC - 1)),
                         deps=[f"p1aq_{g}_{th}_{dc}"] +
                              ([f"rowv_{g}_{th - 1}" if th else f"rowv_{g - 1}_1"]
                               if dc == 0 else []),
                         sig=f"p1sq_{g}_{th}_{dc}")

            def p1_chain(th):
                P.op("scalar", lambda e: e.activation(
                        rows[:, 0:512], scA[0:1, 0:512], AF.Copy, scale=INV_D),
                     deps=[f"stx_{g}_{th}"], sig=f"rowm_{g}_{th}")
                P.op("scalar", lambda e: e.activation(
                        rows[:, 512:1024], rows[:, 0:512], AF.Square), deps=[],
                     sig=f"msq_{g}_{th}")
                P.op("vector", lambda e: e.scalar_tensor_tensor(
                        rows[:, 1024:1536], scA[0:1, 512:1024], INV_D,
                        rows[:, 512:1024], ALU.mult, ALU.subtract),
                     deps=[f"p1sq_{g}_{th}_{DC - 1}", f"msq_{g}_{th}"],
                     sig=f"rowv_{g}_{th}")
                P.op("scalar", lambda e: e.activation(
                        rows[:, 1536:2048], rows[:, 1024:1536], AF.Ln, bias=EPS),
                     deps=[f"rowv_{g}_{th}"])
                P.op("scalar", lambda e: e.activation(
                        rows[:, 512:1024], rows[:, 1536:2048], AF.Exp, scale=-0.5),
                     deps=[])
                P.op("scalar", lambda e: e.activation(
                        rows_bf[:, 0:512], rows[:, 0:512], AF.Copy),
                     deps=[f"bcr_{g}_{th - 1}" if th else f"bcr_{g - 1}_1"])
                P.op("scalar", lambda e: e.activation(
                        rows_bf[:, 512:1024], rows[:, 512:1024], AF.Copy),
                     deps=[], sig=f"rowbf_{g}_{th}")

            def p1_rest(th):
                P.op("tensor", lambda e: e.matmul(
                        scB[:, 0:512], oner[:], rows_bf[:, 0:512],
                        start=True, stop=True),
                     deps=[f"rowbf_{g}_{th}",
                           f"apl_{g}_{th - 1}" if th else f"apl_{g - 1}_1"],
                     sig=f"bcm_{g}_{th}")
                P.op("tensor", lambda e: e.matmul(
                        scB[:, 512:1024], oner[:], rows_bf[:, 512:1024],
                        start=True, stop=True),
                     deps=[], sig=f"bcr_{g}_{th}")
                for dc in range(DC):
                    P.op("vector", lambda e, dc=dc, th=th: e.tensor_sub(
                            xg_t(dc, th), xg_t(dc, th), scB[:, 0:512]),
                         deps=[f"bcm_{g}_{th}", f"p1sq_{g}_{th}_{DC - 1}",
                               f"stx_{g}_{th}"])
                    P.op("vector", lambda e, dc=dc, th=th: e.tensor_mul(
                            xg_t(dc, th), xg_t(dc, th), scB[:, 512:1024]),
                         deps=[f"bcr_{g}_{th}"],
                         sig=([f"apl_{g}_{th}_{dc}", f"apl_{g}_{th}"]
                              if dc == DC - 1 else f"apl_{g}_{th}_{dc}"))
                ocs_ = list(range(18)) if own else [16, 17]
                for oc in ocs_:
                    slot = wload(wq_p[oc], f"wq_{g}_{th}_{oc}")
                    if True:
                        if oc == 17:
                            for j in range(4):
                                bank, bi = next_bank()
                                is_last_grp = (th == 1 and j == 3)
                                for dc in range(DC):
                                    P.op("tensor",
                                         lambda e, dc=dc, th=th, j=j, b=bank, s=slot:
                                         e.matmul(
                                             b[:, 0:128],
                                             asl(A_XN + dc * TG1 + th * 512 + j * 128, 128),
                                             wtile(s, dc),
                                             start=(dc == 0), stop=(dc == DC - 1)),
                                         deps=([f"apl_{g}_{th}", f"wq_{g}_{th}_{oc}",
                                                qkv_bank_free[bi]] if dc == 0 else []),
                                         sig=((["qkvlast_%d" % g, f"vps_{g}_{th}_{j}"]
                                               if is_last_grp else f"vps_{g}_{th}_{j}")
                                              if dc == DC - 1 else None))
                                ev = f"vev_{g}_{th}_{j}"
                                P.op("vector",
                                     lambda e, th=th, j=j, b=bank, gc=gc: e.tensor_copy(
                                         asl(A_V + gc + th * 512 + j * 128, 128),
                                         b[:, 0:128]),
                                     deps=[f"vps_{g}_{th}_{j}"], sig=ev)
                                qkv_bank_free[bi] = ev
                        else:
                            bank, bi = next_bank()
                            first_oc = (oc == ocs_[0])
                            for dc in range(DC):
                                P.op("tensor",
                                     lambda e, dc=dc, th=th, b=bank, s=slot: e.matmul(
                                         b[:], wtile(s, dc), xg_t(dc, th),
                                         start=(dc == 0), stop=(dc == DC - 1)),
                                     deps=([f"apl_{g}_{th}_{dc}"] if first_oc else []) +
                                          (([f"apl_{g}_{th}", f"wq_{g}_{th}_{oc}",
                                             qkv_bank_free[bi]] if not first_oc else
                                            [f"wq_{g}_{th}_{oc}", qkv_bank_free[bi]])
                                           if dc == 0 else []),
                                     sig=(f"qps_{g}_{oc}_{th}" if dc == DC - 1 else None))
                            dst = (asl(A_Q + oc * TQ + gc + th * 512, 512) if oc < 16
                                   else asl(A_K + gc + th * 512, 512))
                            csl = asl(A_COS + gc + th * 512, 512)
                            ssl = asl(A_SIN + gc + th * 512, 512)
                            P.op("vector", lambda e, b=bank, c=csl: e.tensor_mul(
                                    ropeA[:], b[:], c), deps=[f"qps_{g}_{oc}_{th}"])
                            P.op("vector", lambda e, b=bank, s2=ssl: e.tensor_mul(
                                    ropeB[0:64, :], b[64:128, :], s2[0:64, :]), deps=[])
                            ev = f"rpb_{g}_{oc}_{th}"
                            P.op("vector", lambda e, b=bank, s2=ssl: e.tensor_mul(
                                    ropeB[64:128, :], b[0:64, :], s2[64:128, :]),
                                 deps=[], sig=ev)
                            qkv_bank_free[bi] = ev
                            P.op("vector", lambda e, d=dst: e.tensor_sub(
                                    d[0:64, :], ropeA[0:64, :], ropeB[0:64, :]), deps=[])
                            P.op("vector", lambda e, d=dst: e.tensor_add(
                                    d[64:128, :], ropeA[64:128, :], ropeB[64:128, :]),
                                 deps=[], sig=f"rope_{g}_{oc}_{th}")
                    wring_free[slot] = ((f"qkvlast_{g}" if th == 1 else f"vps_{g}_{th}_3")
                                        if oc == 17 else f"qps_{g}_{oc}_{th}")

            p1_sq(0, range(DC))
            p1_stats(0)
            p1_sq(1, range(4))
            p1_chain(0)
            p1_sq(1, range(4, DC))
            p1_stats(1)
            p1_chain(1)
            p1_rest(0)
            p1_rest(1)

        if debug:
            for nm, dst, off, n in (("tk", "d_k", A_K, T), ("tv", "d_v", A_V, T),
                                    ("tq", "d_q", A_Q, 32768), ("txn", "d_xn", A_XN, 16384)):
                dd = {"d_k": d_k, "d_v": d_v, "d_q": d_q, "d_xn": d_xn}[dst]
                P.op("sync", lambda e, dd=dd, off=off, n=n: e.dma_start(
                        out=dd, in_=asl(off, n)),
                     deps=P.snapshot_deps(), dma="dbp")

        P.phase_barrier()

        precast = ([(wo_p[i], wso[i], f"pco_{i}") for i in range(16)] +
                   [(wd_p[i], wsd[i], f"pcd_{i}") for i in range(3)] +
                   [(wg_p[i], wsg[i], f"pcg_{i}") for i in range(ICN)] +
                   [(wu_p[i], wsu[i], f"pcu_{i}") for i in range(ICN)] +
                   [(wd_p[i], wsd[i], f"pcd_{i}") for i in range(3, 16)])
        for j, (src, dst, lbl) in enumerate(precast):
            P.op("gpsimd",
                 lambda e, src=src, dst=dst: e.dma_start(out=dst, in_=src),
                 deps=[precast[j - 8][2]] if j >= 8 else ["exp_111"],
                 dma=f"pc{j % 8}", sig=lbl)

        sc_t = [scA, scB]
        expp = [asl(A_XN + i * 1024, 1024) for i in range(4)]
        avsb = [asl(A_AVSB + i * 512, 512) for i in range(2)]
        k2sb = asl(A_XN, 4096)
        rcpbf2 = [rcpbf[:, 0:512], rcpbf[:, 512:1024]]
        q2sb = [asl(A_XN + 4096 + i * 512, 512) for i in range(2)]
        qksb = [asl(A_XN + 5120 + i * 512, 512) for i in range(2)]

        P.op("scalar", lambda e: e.activation(k2sb, asl(A_K, T), AF.Square),
             deps=[], sig="p2_k2")
        for c in range(8):
            P.op("tensor", lambda e, c=c: e.matmul(
                    smb[:], ones128[:], k2sb[:, c * 512:c * 512 + 512],
                    start=True, stop=True),
                 deps=[f"p2_k2", f"p2_kred_{c - 1}"], sig=f"p2_kmm_{c}")
            P.op("vector", lambda e, c=c: e.reduce_max(
                    mkrow[:, c:c + 1], smb[0:1, :], mybir.AxisListType.X),
                 deps=[f"p2_kmm_{c}"], sig=f"p2_kred_{c}")
        P.op("vector", lambda e: e.reduce_max(
                mkrow[:, 8:9], mkrow[:, 0:8], mybir.AxisListType.X),
             deps=[], sig="p2_mk2")
        P.op("scalar", lambda e: e.activation(
                mkrow[:, 10:11], mkrow[:, 8:9], AF.Ln), deps=["p2_mk2"])
        P.op("scalar", lambda e: e.activation(
                mkrow[:, 9:10], mkrow[:, 10:11], AF.Exp, scale=0.5),
             deps=[], sig="p2_mk")

        def p2a_rows(gh):
            P.op("scalar", lambda e: e.activation(
                    rows[:, 0:512], smb[0:1, :], AF.Ln),
                 deps=[f"p2_nrm_{gh}"], sig=f"p2_ln_{gh}")
            P.op("scalar", lambda e: e.activation(
                    rows[:, 512:1024], rows[:, 0:512], AF.Exp, scale=0.5),
                 deps=[])
            P.op("scalar", lambda e: e.activation(
                    rows[:, 1024:1536], rows[:, 512:1024], AF.Copy,
                    scale=mkrow[:, 9:10], bias=-800.0),
                 deps=[f"p2_maxc_{gh - 1}", "p2_mk"], sig=f"p2_t1_{gh}")
            P.op("vector", lambda e: e.tensor_max(
                    rows[:, 1536:2048], rows[:, 1024:1536], rcb[0:1, :]),
                 deps=[f"p2_t1_{gh}", f"p2_self_{gh}"], sig=f"p2_maxc_{gh}")
            P.op("vector", lambda e, gh=gh: e.tensor_scalar_mul(
                    negcbf[:, (gh % 2) * 512:(gh % 2) * 512 + 512],
                    rows[:, 1536:2048], -1.0),
                 deps=[f"p2_cw_{gh - 2}"], sig=f"p2_negc_{gh}")
            P.op("sync", lambda e, gh=gh: e.dma_start(
                    out=cdram[gh:gh + 1, :],
                    in_=negcbf[:, (gh % 2) * 512:(gh % 2) * 512 + 512]),
                 deps=[f"p2_negc_{gh}"], dma=f"cs{gh % 2}", sig=f"p2_cw_{gh}")

        for gh in range(NQT * NH):
            qt_, h_ = divmod(gh, NH)
            qap = asl(A_Q + h_ * TQ + qt_ * 512, 512)
            kslf = asl(A_K + qt_ * 512, 512)
            P.op("vector", lambda e, q=qap, k=kslf, gh=gh: e.tensor_mul(
                    qksb[gh % 2], q, k),
                 deps=[f"p2_self_{gh - 2}"], sig=f"p2_qk_{gh}")
            P.op("vector", lambda e, q=qap, gh=gh: e.tensor_mul(
                    q2sb[gh % 2], q, q),
                 deps=[f"p2_nrm_{gh - 2}"], sig=f"p2_q2_{gh}")
            P.op("tensor", lambda e, gh=gh: e.matmul(
                    smb[:], ones128[:], q2sb[gh % 2],
                    start=True, stop=True),
                 deps=[f"p2_q2_{gh}", f"p2_ln_{gh - 1}",
                       f"p2_kred_{7}" if gh == 0 else None],
                 sig=f"p2_nrm_{gh}")
            P.op("tensor", lambda e, gh=gh: e.matmul(
                    rcb[:], ones128[:], qksb[gh % 2],
                    start=True, stop=True),
                 deps=[f"p2_qk_{gh}", f"p2_maxc_{gh - 1}"],
                 sig=f"p2_self_{gh}")
            if gh > 0:
                p2a_rows(gh - 1)
        p2a_rows(NQT * NH - 1)

        acc_s = [h1r[:, 0:1024], h1r[:, 1024:2048]]

        def emit_avsum(gh, pr, avb):
            gpr = gh * 16 + pr
            P.op("scalar",
                 lambda e, pr=pr: e.activation(
                     expp[pr % 4], sc_t[pr % 2][:], AF.Exp, scale=EXP_SCALE),
                 deps=[f"sc_{gpr}", f"avp_{gpr - 4}", f"sma_{gpr - 4}"],
                 sig=f"exp_{gpr}")
            for half in range(2):
                tkc = 2 * pr + half
                P.op("tensor",
                     lambda e, tkc=tkc, pr=pr, half=half, b=avb: e.matmul(
                         b[:], asl(A_V + tkc * 128, 128),
                         expp[pr % 4][:, half * 512:half * 512 + 512],
                         start=(pr == 0 and half == 0),
                         stop=(pr == 15 and half == 1)),
                     deps=[f"exp_{gpr}"] if half == 0 else [],
                     sig=(([f"avp_{gpr}", f"av_{gh}"] if pr == 15 else
                           f"avp_{gpr}") if half == 1 else None))
            if pr == 0:
                P.op("vector", lambda e, gh=gh, pr=pr: e.tensor_copy(
                        acc_s[gh % 2], expp[pr % 4]),
                     deps=[f"exp_{gpr}", f"fold_{gh - 2}"], sig=f"sma_{gpr}")
            else:
                P.op("vector", lambda e, gh=gh, pr=pr: e.tensor_add(
                        acc_s[gh % 2], acc_s[gh % 2], expp[pr % 4]),
                     deps=[f"exp_{gpr}"], sig=f"sma_{gpr}")

        def emit_smm(gh):
            P.op("tensor", lambda e, gh=gh: e.matmul(
                    smb[:], ones128[:], q2sb[gh % 2], start=True, stop=True),
                 deps=[f"fold_{gh}", f"rcp_{gh - 1}"] +
                      ([f"p2_ln_{NQT * NH - 1}"] if gh == 0 else []),
                 sig=f"smm_{gh}")

        def emit_rcp(gh):
            P.op("scalar", lambda e: e.activation(
                    rows[:, 0:512], smb[0:1, :], AF.Ln),
                 deps=[f"smm_{gh}"] + ([f"dbgr_{gh - 1}"] if debug else []))
            P.op("scalar", lambda e, gh=gh: e.activation(
                    rcpbf2[gh % 2], rows[:, 0:512], AF.Exp, scale=-1.0),
                 deps=[f"rec_{gh - 2}"], sig=f"rcp_{gh}")

        def emit_fold(gh):
            P.op("vector", lambda e, gh=gh: e.tensor_add(
                    q2sb[gh % 2], acc_s[gh % 2][:, 0:512],
                    acc_s[gh % 2][:, 512:1024]),
                 deps=[f"sma_{gh * 16 + 15}", f"smm_{gh - 2}"],
                 sig=f"fold_{gh}")

        def emit_avcp(gh):
            P.op("vector", lambda e, b=[av0, av1][gh % 2], gh=gh:
                 e.tensor_copy(avsb[gh % 2], b[:]),
                 deps=[f"av_{gh}", f"att_{gh - 2}"] +
                      ([f"dbga_{gh - 2}"] if debug else []),
                 sig=f"avcp_{gh}")

        def emit_recatt(gh):
            qt_, h_ = divmod(gh, NH)
            qap = asl(A_Q + h_ * TQ + qt_ * 512, 512)
            P.op("tensor", lambda e, gh=gh: e.matmul(
                    rcb[:], oner[:], rcpbf2[gh % 2], start=True, stop=True),
                 deps=[f"rcp_{gh}", f"att_{gh - 1}"], sig=f"rec_{gh}")
            P.op("vector", lambda e, q=qap, gh=gh: e.tensor_mul(
                    q, avsb[gh % 2], rcb[:]),
                 deps=[f"rec_{gh}", f"avcp_{gh}"] +
                      ([f"dbgq_{gh}"] if debug else []),
                 sig=f"att_{gh}")

        ncbc = [asl(A_NEGC + i * 512, 512) for i in range(2)]

        def emit_ncload(gh):
            if gh >= NQT * NH:
                return
            P.op("sync", lambda e, gh=gh: e.dma_start(
                    out=ncbc[gh % 2],
                    in_=bass.AP(cdram.tensor, gh * 512, [[0, 128], [1, 512]])),
                 deps=[f"p2_cw_{gh}", f"sc_{(gh - 2) * 16 + 15}"],
                 dma=f"cl{gh % 2}", sig=f"p2_cl_{gh}")
        emit_ncload(0)
        for qt in range(NQT):
            for h in range(NH):
                gh = qt * NH + h
                qap = asl(A_Q + h * TQ + qt * 512, 512)
                avb = [av0, av1][gh % 2]
                emit_ncload(gh + 1)
                if debug:
                    P.op("sync", lambda e, q=qap, gh=gh: e.dma_start(
                            out=d_qrd[gh * 128:(gh + 1) * 128, :], in_=q),
                         deps=[f"p2_cl_{gh}"], dma="dbq", sig=f"dbgq_{gh}")
                for pr in range(16):
                    gpr = gh * 16 + pr
                    if gh > 0:
                        if pr == 0:
                            emit_avsum(gh - 1, 14, [av0, av1][(gh - 1) % 2])
                        if pr == 1:
                            emit_avsum(gh - 1, 15, [av0, av1][(gh - 1) % 2])
                            emit_fold(gh - 1)
                        if pr == 2:
                            emit_avcp(gh - 1)
                        if pr == 4:
                            emit_smm(gh - 1)
                        if pr == 6:
                            emit_rcp(gh - 1)
                    P.op("tensor",
                         lambda e, pr=pr, gh=gh: e.matmul(
                             sc_t[pr % 2][:, 0:512], ident[:], ncbc[gh % 2],
                             start=True, stop=False),
                         deps=[f"exp_{gpr - 2}"] +
                              ([f"p2_cl_{gh}", f"avcp_{gh - 2}"]
                               if pr == 0 else []))
                    P.op("tensor",
                         lambda e, pr=pr, q=qap, tkc=2 * pr: e.matmul(
                             sc_t[pr % 2][:, 0:512],
                             asl(A_K + tkc * 128, 128), q,
                             start=False, stop=True),
                         deps=[])
                    P.op("tensor",
                         lambda e, pr=pr, gh=gh: e.matmul(
                             sc_t[pr % 2][:, 512:1024], ident[:], ncbc[gh % 2],
                             start=True, stop=False),
                         deps=[])
                    P.op("tensor",
                         lambda e, pr=pr, q=qap, tkc=2 * pr + 1: e.matmul(
                             sc_t[pr % 2][:, 512:1024],
                             asl(A_K + tkc * 128, 128), q,
                             start=False, stop=True),
                         deps=[], sig=f"sc_{gpr}")
                    if pr >= 2:
                        emit_avsum(gh, pr - 2, avb)
                if debug and gh > 0:
                    P.op("sync", lambda e, gh=gh - 1: e.dma_start(
                            out=d_rcp[gh:gh + 1, :], in_=rows[:, 0:512]),
                         deps=[f"rcp_{gh}"], dma="dbr", sig=f"dbgr_{gh}")
                    P.op("sync", lambda e, gh=gh - 1: e.dma_start(
                            out=d_av[gh * 128:(gh + 1) * 128, :], in_=avsb[gh % 2]),
                         deps=[f"avcp_{gh}"], dma="dba", sig=f"dbga_{gh}")
                if gh > 0:
                    emit_recatt(gh - 1)
        gl = NQT * NH - 1
        emit_avsum(gl, 14, [av0, av1][gl % 2])
        emit_avsum(gl, 15, [av0, av1][gl % 2])
        emit_fold(gl)
        emit_avcp(gl)
        emit_smm(gl)
        emit_rcp(gl)
        emit_recatt(gl)

        if debug:
            P.op("sync", lambda e: e.dma_start(out=d_att, in_=asl(A_Q, 32768)),
                 deps=P.snapshot_deps(), dma="dbp")

        P.phase_barrier()
        h1bf_s = [asl(A_H1BF + i * 512, 512) for i in range(2)]
        h1sq_s = [asl(A_H1SQ + i * 512, 512) for i in range(2)]
        def emit_h1n(fg):
            fcol = fg * 512
            P.op("tensor", lambda e, fg=fg: e.matmul(
                    scB[:, 0:512], oner[:],
                    ln2bf[:, fg * 512:fg * 512 + 512], start=True, stop=True),
                 deps=[f"ln2_{fg}", f"fact_{fg - 1}_{ICN - 1}"],
                 sig=f"bc2m_{fg}")
            P.op("tensor", lambda e, fg=fg: e.matmul(
                    scB[:, 512:1024], oner[:],
                    ln2bf[:, 2048 + fg * 512:2048 + fg * 512 + 512],
                    start=True, stop=True),
                 deps=[], sig=f"bc2r_{fg}")
            for dc in range(DC):
                P.op("sync" if fg == 0 else "gpsimd",
                     lambda e, dc=dc, fcol=fcol: e.dma_start(
                         out=h1r[:, (dc % 4) * 512:(dc % 4) * 512 + 512],
                         in_=h1T[dc * 128:dc * 128 + 128, fcol:fcol + 512]),
                     deps=[f"h1w_{fg * 16 + dc}",
                           f"h1n_{fg}_{dc - 4}" if dc >= 4
                           else f"h1n_{fg - 1}_{12 + dc}"],
                     dma=(f"h0r{dc % 4}" if fg == 0 else f"hr{dc % 4}"),
                     sig=f"h1rl_{fg}_{dc}")
                P.op("vector", lambda e, dc=dc: e.tensor_sub(
                        ropeA[:], h1r[:, (dc % 4) * 512:(dc % 4) * 512 + 512],
                        scB[:, 0:512]),
                     deps=[f"h1rl_{fg}_{dc}", f"bc2m_{fg}",
                           f"ups_{fg - 1}_{ICN - 1}" if dc == 0 else None])
                P.op("vector", lambda e, dc=dc: e.tensor_mul(
                        asl(A_H1N + dc * 512, 512), ropeA[:],
                        scB[:, 512:1024]),
                     deps=[f"bc2r_{fg}"], sig=f"h1n_{fg}_{dc}")

        def emit_p3stats(tt, oc):
            k = tt * 16 + oc
            lx = f"sx2_{k}"
            lq = f"sq2_{k}"
            P.op("tensor",
                 lambda e, k=k, oc=oc: e.matmul(
                     scA[:, 0:512], ones128[:], h1bf_s[k % 2],
                     start=(oc == 0), stop=(oc == 15)),
                 deps=[f"h1b_{k}"] + ([f"rowm2_{tt - 1}"] if oc == 0 else []),
                 sig=([lx, f"stx2_{tt}"] if oc == 15 else lx))
            P.op("tensor",
                 lambda e, k=k, oc=oc: e.matmul(
                     scA[:, 512:1024], ones128[:], h1sq_s[k % 2],
                     start=(oc == 0), stop=(oc == 15)),
                 deps=[f"h1q_{k}"] + ([f"rowv2_{tt - 1}"] if oc == 0 else []),
                 sig=([lq, f"stq2_{tt}"] if oc == 15 else lq))

        for tt in range(NTT):
            for oc in range(16):
                k = tt * 16 + oc
                slot = wload(wso[oc], f"wo_{k}",
                             extra=([f"pco_{oc}"] if tt == 0 else []))
                bank = [av0, av1][oc % 2]
                P.op("scalar",
                     lambda e, oc=oc, tt=tt: e.dma_start(
                         out=xres[:, (oc % 2) * 512:(oc % 2) * 512 + 512],
                         in_=xt[oc * 128:oc * 128 + 128,
                                tt * 512:tt * 512 + 512]),
                     deps=[f"h1a_{k - 2}"], dma=f"xr{k % 2}", sig=f"xrl_{k}")
                for dc in range(DC):
                    P.op("tensor",
                         lambda e, dc=dc, tt=tt, b=bank, s=slot: e.matmul(
                             b[:], wtile(s, dc),
                             asl(A_Q + dc * TQ + tt * 512, 512),
                             start=(dc == 0), stop=(dc == DC - 1)),
                         deps=([f"wo_{k}", f"h1a_{k - 2}"] if dc == 0 else []),
                         sig=(f"wps_{k}" if dc == DC - 1 else None))
                wring_free[slot] = f"wps_{k}"
                P.op("vector",
                     lambda e, k=k, b=bank: e.tensor_add(
                         h1sb[:, (k % 3) * 512:(k % 3) * 512 + 512], b[:],
                         xres[:, (k % 2) * 512:(k % 2) * 512 + 512]),
                     deps=[f"wps_{k}", f"xrl_{k}", f"h1w_{k - 3}",
                           f"h1b_{k - 3}"],
                     sig=f"h1a_{k}")
                P.op("scalar",
                     lambda e, k=k: e.activation(
                         h1sq_s[k % 2],
                         h1sb[:, (k % 3) * 512:(k % 3) * 512 + 512], AF.Square),
                     deps=[f"h1a_{k}", f"sq2_{k - 2}" if k >= 2 else None],
                     sig=f"h1q_{k}")
                P.op("scalar",
                     lambda e, k=k: e.activation(
                         h1bf_s[k % 2],
                         h1sb[:, (k % 3) * 512:(k % 3) * 512 + 512], AF.Copy),
                     deps=[f"sx2_{k - 2}" if k >= 2 else None],
                     sig=f"h1b_{k}")
                P.op("sync",
                     lambda e, k=k, oc=oc, tt=tt: e.dma_start(
                         out=h1T[oc * 128:oc * 128 + 128,
                                 tt * 512:tt * 512 + 512],
                         in_=h1sb[:, (k % 3) * 512:(k % 3) * 512 + 512]),
                     deps=[f"h1a_{k}"], dma=f"hs{k % 3}", sig=f"h1w_{k}")
                if oc > 0:
                    emit_p3stats(tt, oc - 1)
                if oc == 15:
                    emit_p3stats(tt, 15)
            P.op("scalar", lambda e: e.activation(
                    rows[:, 0:512], scA[0:1, 0:512], AF.Copy, scale=INV_D),
                 deps=[f"stx2_{tt}"], sig=f"rowm2_{tt}")
            P.op("scalar", lambda e: e.activation(
                    rows[:, 512:1024], rows[:, 0:512], AF.Square), deps=[],
                 sig=f"msq2_{tt}")
            P.op("vector", lambda e: e.scalar_tensor_tensor(
                    rows[:, 1024:1536], scA[0:1, 512:1024], INV_D,
                    rows[:, 512:1024], ALU.mult, ALU.subtract),
                 deps=[f"stq2_{tt}", f"msq2_{tt}"], sig=f"rowv2_{tt}")
            P.op("scalar", lambda e: e.activation(
                    rows[:, 1536:2048], rows[:, 1024:1536], AF.Ln, bias=EPS),
                 deps=[f"rowv2_{tt}"])
            P.op("scalar", lambda e: e.activation(
                    rows[:, 512:1024], rows[:, 1536:2048], AF.Exp, scale=-0.5),
                 deps=[])
            P.op("scalar", lambda e, tt=tt: e.activation(
                    ln2bf[:, tt * 512:tt * 512 + 512], rows[:, 0:512], AF.Copy),
                 deps=[])
            P.op("scalar", lambda e, tt=tt: e.activation(
                    ln2bf[:, 2048 + tt * 512:2048 + tt * 512 + 512],
                    rows[:, 512:1024], AF.Copy),
                 deps=[], sig=f"ln2_{tt}")
            if tt == 0:
                emit_h1n(0)

        P.phase_barrier()
        silu_s = [asl(A_AVSB + i * 512, 512) for i in range(2)]
        wd_free = [None, None, None]

        for fg in range(NFG):
            fcol = fg * 512
            for ic in range(ICN):
                sg = wload(wsg[ic], f"wg_{fg}_{ic}",
                           extra=([f"pcg_{ic}"] if fg == 0 else []))
                su = wload(wsu[ic], f"wu_{fg}_{ic}",
                           extra=([f"pcu_{ic}"] if fg == 0 else []))
                if ic == 1:
                    for oc2 in range(3):
                        wdfr0 = wd_free[oc2 % 3]
                        wdfr0 = wdfr0 if isinstance(wdfr0, list) else [wdfr0]
                        if fg == 0:
                            wdfr0 = wdfr0 + [f"pcd_{oc2}"]
                        P.op("gpsimd",
                             lambda e, oc=oc2: e.dma_start(
                                 out=asl(WD_SLOTS[oc % 3], INTER),
                                 in_=wsd[oc]),
                             deps=wdfr0, dma=f"wd{oc2 % 3}",
                             sig=f"wdl_{fg}_{oc2}")
                gb = scA[:, (ic % 2) * 512:(ic % 2) * 512 + 512]
                ub = scB[:, (ic % 2) * 512:(ic % 2) * 512 + 512]
                for dc in range(DC):
                    P.op("tensor",
                         lambda e, dc=dc, s=sg, b=gb: e.matmul(
                             b, wtile(s, dc), asl(A_H1N + dc * 512, 512),
                             start=(dc == 0), stop=(dc == DC - 1)),
                         deps=([f"wg_{fg}_{ic}",
                                f"silu_{fg}_{ic - 2}" if ic >= 2
                                else f"silu_{fg - 1}_{ICN - 2 + ic}"]
                               if dc == 0 else []) +
                              ([f"h1n_{fg}_{dc}"] if ic == 0 else []),
                         sig=(f"gps_{fg}_{ic}" if dc == DC - 1 else None))
                for dc in range(DC):
                    P.op("tensor",
                         lambda e, dc=dc, s=su, b=ub: e.matmul(
                             b, wtile(s, dc), asl(A_H1N + dc * 512, 512),
                             start=(dc == 0), stop=(dc == DC - 1)),
                         deps=([f"wu_{fg}_{ic}", f"h1n_{fg}_{DC - 1}",
                                f"fact_{fg}_{ic - 2}" if ic >= 2 else None]
                               if dc == 0 else []),
                         sig=(f"ups_{fg}_{ic}" if dc == DC - 1 else None))
                wring_free[sg] = f"gps_{fg}_{ic}"
                wring_free[su] = f"ups_{fg}_{ic}"
                P.op("scalar",
                     lambda e, ic=ic, b=gb: e.activation(
                         silu_s[ic % 2], b, AF.Silu, scale=gs[:, ic:ic + 1]),
                     deps=[f"gps_{fg}_{ic}"], sig=f"silu_{fg}_{ic}")
                P.op("vector",
                     lambda e, ic=ic, b=ub: e.scalar_tensor_tensor(
                         asl(A_ACT + ic * 512, 512), b, us[:, ic:ic + 1],
                         silu_s[ic % 2], ALU.mult, ALU.mult),
                     deps=[f"ups_{fg}_{ic}", f"silu_{fg}_{ic}",
                           f"fps_{fg - 1}_15" if ic == 0 else None],
                     sig=f"fact_{fg}_{ic}")
            if fg + 1 < NFG:
                emit_h1n(fg + 1)
            for oc in range(16):
                ko = fg * 16 + oc
                if oc >= 3:
                    wdfr = wd_free[oc % 3]
                    wdfr = wdfr if isinstance(wdfr, list) else [wdfr]
                    if fg == 0:
                        wdfr = wdfr + [f"pcd_{oc}"]
                    P.op("gpsimd",
                         lambda e, oc=oc: e.dma_start(
                             out=asl(WD_SLOTS[oc % 3], INTER),
                             in_=wsd[oc]),
                         deps=wdfr, dma=f"wd{oc % 3}",
                         sig=f"wdl_{fg}_{oc}")
                bank = [av0, av1][oc % 2]
                for ic in range(ICN):
                    P.op("tensor",
                         lambda e, ic=ic, oc=oc, b=bank: e.matmul(
                             b[:], asl(WD_SLOTS[oc % 3] + ic * 128, 128),
                             asl(A_ACT + ic * 512, 512),
                             start=(ic == 0), stop=(ic == ICN - 1)),
                         deps=([f"wdl_{fg}_{oc}", f"fact_{fg}_{ICN - 1}",
                                f"yv_{ko - 2}"] if ic == 0 else []),
                         sig=(f"fps_{fg}_{oc}" if ic == ICN - 1 else None))
                wd_free[oc % 3] = f"fps_{fg}_{oc}"
                P.op("sync",
                     lambda e, oc=oc, fcol=fcol: e.dma_start(
                         out=h1r[:, 2048 + (oc % 2) * 512:
                                 2048 + (oc % 2) * 512 + 512],
                         in_=h1T[oc * 128:oc * 128 + 128, fcol:fcol + 512]),
                     deps=[f"yv_{ko - 2}"], dma=f"h2{ko % 2}", sig=f"h1rl2_{ko}")
                P.op("vector",
                     lambda e, oc=oc, b=bank: e.scalar_tensor_tensor(
                         ysb[:, (oc % 2) * 512:(oc % 2) * 512 + 512], b[:],
                         ds[:, oc:oc + 1],
                         h1r[:, 2048 + (oc % 2) * 512:2048 + (oc % 2) * 512 + 512],
                         ALU.mult, ALU.add),
                     deps=[f"fps_{fg}_{oc}", f"h1rl2_{ko}", f"yw_{ko - 2}"],
                     sig=f"yv_{ko}")
                P.op("sync",
                     lambda e, oc=oc, fcol=fcol: e.dma_start(
                         out=yt[oc * 128:oc * 128 + 128, fcol:fcol + 512],
                         in_=ysb[:, (oc % 2) * 512:(oc % 2) * 512 + 512]),
                     deps=[f"yv_{ko}"], dma=f"ys{ko % 2}", sig=f"yw_{ko}")

        P.op("sync", lambda e: e.nop(), deps=P.snapshot_deps())

        FIN = P.finalize()
        sems = {s: E(nc.semaphore(s)) for s in P.counts}
        global LAST_PLAN
        LAST_PLAN = P
        with nc.Block() as block:
            def mk(engname):
                def body(eng):
                    for waits, fn, inc in FIN[engname]:
                        for s, v in waits:
                            eng.wait_ge(sems[s], v)
                        inst = fn(eng)
                        if inc is not None:
                            sname, amt = inc
                            inst.then_inc(sems[sname], amt)
                return body
            block.tensor(mk("tensor"))
            block.vector(mk("vector"))
            block.scalar(mk("scalar"))
            block.sync(mk("sync"))
            block.gpsimd(mk("gpsimd"))

    return nc



def _prep_w(W):
    O, I = W.shape
    return np.ascontiguousarray(
        W.reshape(O // 128, 128, I // 128, 128).transpose(0, 3, 2, 1)
    ).reshape(O // 128, 128, I)


_CACHED = {}


def kernel(L0_attn_x, L0_attn_w_attn, L0_attn_w_out, L0_ffn_gate_w,
           L0_ffn_up_w, L0_ffn_down_w, L0_ffn_gate_s, L0_ffn_up_s,
           L0_ffn_down_s):
    x = np.asarray(L0_attn_x)
    B = x.shape[0]
    bf = ml_dtypes.bfloat16

    pos = (10000.0 ** ((-2.0 * np.arange(0, HD, 2) - 1.0) / HD)).astype(np.float32)
    ang = np.arange(T, dtype=np.float32)[:, None] * pos[None, :]
    ang = np.concatenate([ang, ang], axis=1)
    cosT = np.ascontiguousarray(np.cos(ang).T)
    sinT = np.ascontiguousarray(np.sin(ang).T)

    wq = _prep_w(np.asarray(L0_attn_w_attn))
    wo = _prep_w(np.asarray(L0_attn_w_out))
    wg = _prep_w(np.asarray(L0_ffn_gate_w))
    wu = _prep_w(np.asarray(L0_ffn_up_w))
    wd = _prep_w(np.asarray(L0_ffn_down_w))
    gsv = np.ascontiguousarray(np.asarray(L0_ffn_gate_s).reshape(ICN, 128).T)
    usv = np.ascontiguousarray(np.asarray(L0_ffn_up_s).reshape(ICN, 128).T)
    dsv = np.ascontiguousarray(np.asarray(L0_ffn_down_s).reshape(16, 128).T)
    cval = np.tile(np.array([CONSTS], np.float32), (128, 1))
    onec = np.ones((128, 1), bf)
    oner = np.ones((1, 128), bf)
    ones128 = np.ones((128, 128), bf)
    ident = np.eye(128).astype(bf)

    nc = build_program()

    in_maps = []
    for c in range(8):
        b, hh = c // 2, c % 2
        if hh == 0:
            xtc = np.ascontiguousarray(x[b].T)
            cosc, sinc = cosT, sinT
        else:
            xtc = np.ascontiguousarray(np.concatenate(
                [x[b, 2048:4096].T, x[b, 0:2048].T], axis=1))
            cosc = np.ascontiguousarray(np.concatenate(
                [cosT[:, 2048:4096], cosT[:, 0:2048]], axis=1))
            sinc = np.ascontiguousarray(np.concatenate(
                [sinT[:, 2048:4096], sinT[:, 0:2048]], axis=1))
        in_maps.append(dict(
            xt=xtc, cosb=cosc.astype(bf), sinb=sinc.astype(bf),
            wq_p=wq, wo_p=wo, wg_p=wg, wu_p=wu, wd_p=wd,
            gs_d=gsv, us_d=usv, ds_d=dsv, cvals=cval,
            onec_d=onec, oner_d=oner, ones128_d=ones128, ident_d=ident))

    res = run_bass_kernel_spmd(nc, in_maps, core_ids=list(range(8)))

    out = np.empty((B, T, D), np.float32)
    for c in range(8):
        b, hh = c // 2, c % 2
        out[b, hh * 2048:(hh + 1) * 2048, :] = res.results[c]["yt"].T
    return out
